# version 01
# baseline (speedup 1.0000x reference)
import contextlib
import numpy as np
import ml_dtypes
import concourse.bass as bass
import concourse.mybir as mybir
from concourse.bass_utils import run_bass_kernel_spmd

F32 = mybir.dt.float32
F32R = mybir.dt.float32r
BF16 = mybir.dt.bfloat16
F16 = mybir.dt.float16
AF = mybir.ActivationFunctionType
ALU = mybir.AluOpType
AX = mybir.AxisListType

B, S, D = 16, 2048, 1024
NCORE = 8
BC = B // NCORE
T = BC * S
NT = T // 128
TPB = S // 128
II = 2816
NI = II // 128
ND = D // 128
H, HD = 16, 64
NPAIR = H // 2
EPS = 1e-5
QMAX = 127.0
R16 = 1536.0
CLIP_LO, CLIP_HI = R16 - 128.0, R16 + 127.0
NEGBIG = -1.0e9
EXPBIAS = -10.0

STAGE = 3
UPTO = 99
PUPTO = 99
NWT = 99
QSTEPS = 'LQTES'
DBG = 0
LAST_RESULTS = None
DELAY = 0

SBUF_LIMIT = 229344
ARENA_BASE = 56 * 1024


class Sem:
    __slots__ = ("name", "n", "h")

    def __init__(self, name):
        self.name, self.n, self.h = name, 0, None


class Queue:

    def __init__(self, name, nsems):
        self.name = name
        self.subs = [Sem(f"{name}{j}") for j in range(nsems)]
        self.idx = 0

    def tok(self):
        j = self.idx % len(self.subs)
        self.idx += 1
        s = self.subs[j]
        s.n += 16
        return (s, s.n)

    def snapshot(self):
        return [(s, s.n) for s in self.subs if s.n > 0]


class Builder:
    ENGINES = ("sync", "vector", "scalar", "tensor", "gpsimd")

    def __init__(self, nc):
        self.nc = nc
        self.code = {e: [] for e in self.ENGINES}
        self.sems = {}
        self.nbar = 0
        self.base = ARENA_BASE
        self.arena = ARENA_BASE

    def sem(self, name):
        s = Sem(name)
        self.sems[name] = s
        return s

    def em(self, eng, fn):
        self.code[eng].append(fn)

    def set_base(self, b):
        self.base = (b + 31) & ~31
        self.arena = self.base

    def phase_reset(self, keep=0):
        self.arena = self.base + keep

    def aa(self, name, shape, dtype):
        sz = int(np.prod(shape[1:])) * mybir.dt.size(dtype)
        off = (self.arena + 31) & ~31
        self.arena = off + sz
        assert self.arena <= SBUF_LIMIT, (name, self.arena)
        return self.nc.alloc_sbuf_tensor_at(name, shape, dtype, offset=off)

    def barrier(self):
        self.nbar += 1
        tgt = 5 * self.nbar
        snap = []
        for q in self.queues:
            snap += q.snapshot()
        bar = self.sems["bar"]
        first = self.nbar == 1
        for eng in self.ENGINES:
            def f(e, t=tgt, snap=tuple(snap), eng=eng, first=first):
                if eng == "tensor" and not first:
                    e.transpose(self.dummy_ps, self.dummy_id,
                                self.dummy_id).then_inc(bar.h, 1)
                elif eng == "vector":
                    e.memset(self.dmy_v, 0.0).then_inc(bar.h, 1)
                elif eng == "scalar":
                    e.memzero(self.dmy_s).then_inc(bar.h, 1)
                elif eng == "gpsimd":
                    e.memset(self.dmy_g, 0.0).then_inc(bar.h, 1)
                else:
                    e.nop().then_inc(bar.h, 1)
                e.wait_ge(bar.h, t)
                for s, v in snap:
                    e.wait_ge(s.h, v)
            self.em(eng, f)
        bar.n = tgt


def build_program():
    nc = bass.Bass("TRN2", target_bir_lowering=False,
                   detect_race_conditions=False)
    bd = Builder(nc)

    dm = {}

    def dr(name, shape, dtype, kind="Internal"):
        dm[name] = nc.dram_tensor(name, shape, dtype, kind=kind)
        return dm[name]

    dr("x", [NT, 128, D], F32, "ExternalInput")
    dr("wg", [II, D], F32, "ExternalInput")
    dr("wu", [II, D], F32, "ExternalInput")
    dr("wd", [D, II], F32, "ExternalInput")
    dr("clip", [1, 2], F32, "ExternalInput")
    if STAGE >= 2:
        dr("wq", [NPAIR, ND, 128, 128], F32R, "ExternalInput")
        dr("wk", [NPAIR, ND, 128, 128], F32R, "ExternalInput")
        dr("wv", [NPAIR, ND, 128, 128], F32R, "ExternalInput")
        dr("wo", [ND, 128, D], F32R, "ExternalInput")
        dr("wob", [1, D], F32, "ExternalInput")
        dr("t1", [128, S], F32, "ExternalInput")
        dr("t2", [128, S], F32, "ExternalInput")
        dr("idr", [128, 128], F32R, "ExternalInput")
        dr("trimask", [128, 1024], F32R, "ExternalInput")
    dr("idbf", [128, 128], BF16, "ExternalInput")
    dr("out", [NT, 128, D], F32, "ExternalOutput")
    if DBG == 3:
        dr("dbgout", [128, 4096], F32, "ExternalOutput")
    dr("X", [NT, 128, D], F32)
    dr("G2", [NI, 128, T], F32)
    dr("WGT", [NI, 128, ND * 128], BF16)
    dr("WUT", [NI, 128, ND * 128], BF16)
    dr("IWD", [128, NI, D], BF16)
    dr("ACC", [ND, 128, S], F32R)
    dr("gbounce", [1, 128], F32)
    dr("rowbounce", [1, D], F32)
    dr("scbounce", [1, 8], F32)
    dr("sdbounce", [128, ND], F32)
    for i in range(4):
        dr(f"ccin{i}", [1, 8], F32)
        dr(f"ccout{i}", [1, 8], F32)

    es = contextlib.ExitStack()
    qs = Queue("qs", 4)
    qg = Queue("qg", 4)
    qc = Queue("qc", 1)
    bd.queues = [qs, qg, qc]
    for q in bd.queues:
        for s in q.subs:
            s.h = es.enter_context(nc.semaphore(s.name))
    for name in ("tv", "vt", "sv", "vv", "ss", "gg", "bar"):
        s = bd.sem(name)
        s.h = es.enter_context(nc.semaphore(name))
    SM = bd.sems
    tv, vt, sv, vv, ss, gg = (SM["tv"], SM["vt"], SM["sv"], SM["vv"],
                              SM["ss"], SM["gg"])

    def W(e, tok):
        e.wait_ge(tok[0].h, tok[1])

    st = {}

    def sb(name, shape, dtype):
        st[name] = nc.alloc_sbuf_tensor(name, shape, dtype)
        return st[name]

    sb("c_t1", [128, S], F32)
    sb("c_t2", [128, S], F32)
    sb("c_idbf", [128, 128], BF16)
    sb("c_idr", [128, 128], F32R)
    sb("c_tri", [128, 1024], F32R)
    sb("c_eps", [128, 1], F32)
    sb("c_neg10", [128, 1], F32)
    sb("c_ones", [1, 128], F32)
    sb("c_onesr", [128, 16], F32R)
    sb("c_ones16", [128, 16], F32)
    sb("sigm", [1, 2], F32)
    sb("negm", [128, NT], F32)
    sb("var", [128, NT], F32)
    sb("rstd", [128, NT], F32)
    sb("cq", [128, NT], F32)
    sb("m1536", [128, NT], F32)
    sb("sc_tmp", [128, NT], F32)
    sb("mx", [128, NT], F32)
    sb("mn", [128, NT], F32)
    sb("bnbuf", [128, 12], F32)
    sb("agbuf", [128, 2], F32)
    sb("amax1", [128, 1], F32)
    sb("g2mx", [128, 1], F32)
    sb("gath", [1, 128], F32)
    sb("ccpad", [1, 8], F32)
    sb("ccres", [1, 8], F32)
    sb("swg", [128, NI], F32)
    sb("swu", [128, NI], F32)
    sb("sdd", [128, ND], F32)
    sb("srow", [1, D], F32)
    sb("sd_b", [128, D], F32)
    sb("sdb_f", [128, D], F32)
    sb("cg", [128, NI], F32)
    sb("cu", [128, NI], F32)
    sb("sc_bank", [128, 8], F32)
    sb("scrow", [1, 8], F32)
    sb("wob_row", [1, D], F32)
    sb("tiny", [1, 8], F32)
    sb("onesall", [128, 64], F32)

    bd.set_base(nc._sbuf_addr_for_side("left"))
    assert SBUF_LIMIT - bd.base >= 160 * 1024, f"arena too small: {bd.base}"

    ps_f = es.enter_context(nc.psum_tensor("ps_f", [128, 6, 512], F32))
    ps_b = es.enter_context(nc.psum_tensor("ps_b", [128, 2, 1024], BF16))
    bd.dummy_ps = ps_f[:, 5, 0:64].bitcast(BF16)
    bd.dummy_id = st["c_idbf"][:]
    bd.dmy_v = st["bnbuf"][0:1, 0:1]
    bd.dmy_s = st["bnbuf"][0:1, 2:3]
    bd.dmy_g = st["bnbuf"][0:1, 4:5]

    if DELAY:
        dstart = gg.n + 1
        gg.n = dstart

        def gdelay(e, dstart=dstart):
            for _ in range(1000):
                e.nop(cycle_cnt=60000)
            e.nop().then_inc(gg.h, 1)
        bd.em("gpsimd", gdelay)
        for eng in ("sync", "vector", "scalar", "tensor"):
            bd.em(eng, lambda e, t=dstart: e.wait_ge(gg.h, t))

    def vhz():
        t = vv.n + 1
        vv.n = t
        bd.em("vector", lambda e, t=t: (
            e.memset(bd.dmy_v, 0.0).then_inc(vv.h, 1),
            e.wait_ge(vv.h, t)))

    def global_max(src_fn, idx):
        vready = vv.n
        t1_ = qg.tok()
        t2_ = qg.tok()

        def g1(e, vready=vready, t1_=t1_, t2_=t2_):
            e.wait_ge(vv.h, vready)
            e.dma_start(out=dm["gbounce"].ap(), in_=src_fn()
                        ).then_inc(t1_[0].h, 16)
            W(e, t1_)
            e.dma_start(out=st["gath"][:], in_=dm["gbounce"].ap()
                        ).then_inc(t2_[0].h, 16)
            W(e, t2_)
        bd.em("gpsimd", g1)

        vdone = vv.n + 2
        vv.n = vdone

        def v1(e, t2_=t2_, vdone=vdone):
            W(e, t2_)
            e.memset(st["ccpad"][:], 0.0).then_inc(vv.h, 1)
            e.wait_ge(vv.h, vdone - 1)
            e.reduce_max(st["ccpad"][0:1, 0:1], st["gath"][:],
                         axis=AX.X).then_inc(vv.h, 1)
        bd.em("vector", v1)

        t3_ = qg.tok()
        t4_ = qg.tok()
        gdone = gg.n + 1
        gg.n = gdone

        def g2(e, vdone=vdone, t3_=t3_, t4_=t4_, gdone=gdone, idx=idx):
            e.wait_ge(vv.h, vdone)
            e.dma_start(out=dm[f"ccin{idx}"].ap(), in_=st["ccpad"][:]
                        ).then_inc(t3_[0].h, 16)
            W(e, t3_)
            e.collective_compute(
                "AllReduce", ALU.max, replica_groups=[list(range(NCORE))],
                ins=[dm[f"ccin{idx}"].ap()], outs=[dm[f"ccout{idx}"].ap()],
            ).then_inc(gg.h, 1)
            e.wait_ge(gg.h, gdone)
            e.dma_start(out=st["ccres"][:], in_=dm[f"ccout{idx}"].ap()
                        ).then_inc(t4_[0].h, 16)
            W(e, t4_)
        bd.em("gpsimd", g2)
        return t4_

    def repl_scalars(row_fn, ncols, dst0, vwait):
        ta = qg.tok()
        tb = qg.tok()

        def g(e, vwait=vwait, ncols=ncols, dst0=dst0, ta=ta, tb=tb):
            e.wait_ge(vv.h, vwait)
            e.dma_start(out=dm["scbounce"].ap()[0:1, 0:ncols],
                        in_=row_fn()).then_inc(ta[0].h, 16)
            W(e, ta)
            e.dma_start(out=st["sc_bank"][:, dst0:dst0 + ncols],
                        in_=bass.AP(dm["scbounce"], 0, [[0, 128], [1, ncols]])
                        ).then_inc(tb[0].h, 16)
            W(e, tb)
        bd.em("gpsimd", g)

    def prologue():
        consts = [("c_idbf", "idbf"), ("sigm", "clip")]
        if STAGE >= 2:
            consts += [("c_t1", "t1"), ("c_t2", "t2"), ("c_idr", "idr"),
                       ("c_tri", "trimask"), ("wob_row", "wob")]
        ctoks = [qc.tok() for _ in consts]

        def sc0(e, ctoks=tuple(ctoks)):
            for (dn, sn), tk in zip(consts, ctoks):
                e.dma_start(out=st[dn][:], in_=dm[sn].ap()
                            ).then_inc(tk[0].h, 16)
            W(e, ctoks[-1])
            e.activation(st["sigm"][:], st["sigm"][:], AF.Sigmoid)
        bd.em("scalar", sc0)

        def v0(e):
            e.memset(st["c_eps"][:], EPS)
            e.memset(st["c_neg10"][:], EXPBIAS)
            e.memset(st["c_ones"][:], 1.0)
            e.memset(st["c_ones16"][:], 1.0)
            e.memset(st["onesall"][:], 1.0)
            e.tensor_copy(st["c_onesr"][:], st["c_ones16"][:])
        bd.em("vector", v0)
        bd.barrier()

        bd.phase_reset()
        wstage = bd.aa("p0_wstage", [128, 2, II], F32)
        w16 = bd.aa("p0_w16", [128, II], F16)
        wiq = bd.aa("p0_wiq", [128, 2, II], BF16)
        wtmp = bd.aa("p0_wtmp", [128, 2, II], BF16)
        amaxw = bd.aa("p0_amaxw", [128, 1], F32)
        r127 = bd.aa("p0_r127", [128, 1], F32)

        wsfree = [0, 0]
        wiqfree = [0, 0]
        wtmfree = [0, 0]
        psfree = [0, 0]

        def qwt(src_ap_fn, n_tiles, width, nchunk, scale_dst, dst_fn):
            for it in range(min(n_tiles, NWT)):
                buf = it % 2
                tq = qs.tok()

                def dmain(e, it=it, buf=buf, wf=wsfree[buf], width=width,
                          src_ap_fn=src_ap_fn, tq=tq):
                    e.wait_ge(vv.h, wf)
                    e.dma_start(out=wstage[:, buf, 0:width],
                                in_=src_ap_fn(it)).then_inc(tq[0].h, 16)
                bd.em("sync", dmain)

                if "Q" not in QSTEPS:
                    continue
                vq1 = vv.n + 3
                vv.n = vq1

                def vquant(e, it=it, buf=buf, tq=tq, width=width,
                           scale_dst=scale_dst, vq1=vq1):
                    W(e, tq)
                    w = wstage[:, buf, 0:width]
                    e.reduce_max(amaxw[:], w, axis=AX.X,
                                 apply_absolute_value=True).then_inc(vv.h, 1)
                    e.wait_ge(vv.h, vq1 - 2)
                    e.reciprocal(r127[:], amaxw[:])
                    e.tensor_scalar(st[scale_dst][:, it:it + 1], amaxw[:],
                                    1.0 / QMAX, None, ALU.mult
                                    ).then_inc(vv.h, 1)
                    e.wait_ge(vv.h, vq1 - 1)
                    e.tensor_scalar(r127[:], r127[:], QMAX, None, ALU.mult
                                    ).then_inc(vv.h, 1)
                bd.em("vector", vquant)
                vhz()

                vq2 = vv.n + 3
                vv.n = vq2

                def vquant2(e, buf=buf, width=width, wqf=wiqfree[buf],
                            vq2=vq2):
                    e.wait_ge(tv.h, wqf)
                    w = wstage[:, buf, 0:width]
                    e.tensor_scalar(w16[:, 0:width], w, r127[:], R16,
                                    ALU.mult, ALU.add).then_inc(vv.h, 1)
                    e.wait_ge(vv.h, vq2 - 2)
                    e.tensor_scalar(w16[:, 0:width], w16[:, 0:width],
                                    CLIP_LO, CLIP_HI, ALU.max,
                                    ALU.min).then_inc(vv.h, 1)
                    e.wait_ge(vv.h, vq2 - 1)
                    e.tensor_scalar(wiq[:, buf, 0:width], w16[:, 0:width],
                                    -R16, None, ALU.add).then_inc(vv.h, 1)
                bd.em("vector", vquant2)
                wsfree[buf] = vq2

                if "T" not in QSTEPS:
                    continue
                ngr = (nchunk + 7) // 8
                for g in range(ngr):
                    cnt = min(8, nchunk - g * 8)
                    ttgt = tv.n + 1
                    tv.n = ttgt

                    def tt(e, buf=buf, g=g, cnt=cnt, vq2=vq2, pf=psfree[g % 2]):
                        e.wait_ge(vv.h, max(vq2, pf))
                        inst = None
                        for c in range(cnt):
                            inst = e.transpose(
                                ps_b[:, g % 2, c * 128:(c + 1) * 128],
                                wiq[:, buf, (g * 8 + c) * 128:
                                    (g * 8 + c + 1) * 128],
                                st["c_idbf"][:])
                        inst.then_inc(tv.h, 1)
                    bd.em("tensor", tt)

                    if "E" not in QSTEPS:
                        continue
                    vev = vv.n + 1
                    vv.n = vev

                    def vevf(e, buf=buf, g=g, cnt=cnt, ttgt=ttgt,
                             wmf=wtmfree[buf]):
                        e.wait_ge(tv.h, ttgt)
                        if wmf != 0:
                            W(e, wmf)
                        e.tensor_copy(
                            wtmp[:, buf, g * 8 * 128:(g * 8 + cnt) * 128],
                            ps_b[:, g % 2, 0:cnt * 128]).then_inc(vv.h, 1)
                    bd.em("vector", vevf)
                    psfree[g % 2] = vev
                wiqfree[buf] = tv.n

                if "S" not in QSTEPS:
                    continue
                tqg = qg.tok()
                vneed = vv.n

                def gstore(e, it=it, buf=buf, vneed=vneed, dst_fn=dst_fn,
                           nchunk=nchunk, tqg=tqg):
                    e.wait_ge(vv.h, vneed)
                    dst = dst_fn(it)
                    innp = wtmp[:, buf, 0:nchunk * 128]
                    if len(dst.shape) == 3:
                        innp = innp.rearrange("p (c n) -> p c n", n=128)
                    e.dma_start(out=dst, in_=innp).then_inc(tqg[0].h, 16)
                bd.em("gpsimd", gstore)
                wtmfree[buf] = tqg

        wg_ap = dm["wg"].ap().rearrange("(t p) d -> t p d", p=128)
        wu_ap = dm["wu"].ap().rearrange("(t p) d -> t p d", p=128)
        wd_ap = dm["wd"].ap().rearrange("(t p) d -> t p d", p=128)
        if PUPTO < 1:
            bd.barrier()
            return
        qwt(lambda it: wg_ap[it], NI, D, ND, "swg",
            lambda it: dm["WGT"].ap()[it])
        if PUPTO >= 2:
            qwt(lambda it: wu_ap[it], NI, D, ND, "swu",
                lambda it: dm["WUT"].ap()[it])
        if PUPTO >= 3:
            qwt(lambda it: wd_ap[it], ND, II, NI, "sdd",
                lambda it: dm["IWD"].ap()[:, :, it * 128:(it + 1) * 128])
        bd.barrier()
        if PUPTO < 4:
            return

        t1_ = qg.tok()
        t2_ = qg.tok()

        def gsd(e, t1_=t1_, t2_=t2_):
            e.dma_start(out=dm["sdbounce"].ap(), in_=st["sdd"][:]
                        ).then_inc(t1_[0].h, 16)
            W(e, t1_)
            e.dma_start(out=st["srow"][:],
                        in_=bass.AP(dm["sdbounce"], 0,
                                    [[1, 1], [1, ND], [ND, 128]]),
                        allow_slow_non_contiguous=True
                        ).then_inc(t2_[0].h, 16)
            W(e, t2_)
        bd.em("gpsimd", gsd)

        t3_ = qg.tok()
        t4_ = qg.tok()

        def gsdb(e, t2_=t2_, t3_=t3_, t4_=t4_):
            W(e, t2_)
            e.dma_start(out=dm["rowbounce"].ap(), in_=st["srow"][:]
                        ).then_inc(t3_[0].h, 16)
            W(e, t3_)
            e.dma_start(out=st["sd_b"][:],
                        in_=bass.AP(dm["rowbounce"], 0, [[0, 128], [1, D]])
                        ).then_inc(t4_[0].h, 16)
            W(e, t4_)
        bd.em("gpsimd", gsdb)
        bd.barrier()

    def ffn(tag, src, dst, cc0):
        _u = [0]

        def done():
            _u[0] += 1
            return UPTO <= _u[0]

        bd.phase_reset()
        xst = bd.aa(f"{tag}_xst", [128, 2, 2, D], F32)
        sq2 = bd.aa(f"{tag}_sq2", [128, 2, D], F32)
        sqs = bd.aa(f"{tag}_sqs", [128, NT], F32)
        stfree = [0, 0]
        sqfree = [0, 0]
        for tp in range(NT // 2):
            buf = tp % 2
            tq = qs.tok()

            def dmx(e, tp=tp, buf=buf, sf=stfree[buf], src=src, tq=tq):
                e.wait_ge(vv.h, sf)
                e.dma_start(out=xst[:, buf],
                            in_=src.ap()[2 * tp:2 * tp + 2].transpose([1, 0, 2])
                            ).then_inc(tq[0].h, 16)
            bd.em("sync", dmx)

            for k in range(2):
                t = 2 * tp + k
                sdone = ss.n + 1
                ss.n = sdone

                def sstat(e, t=t, buf=buf, k=k, tq=tq, sqf=sqfree[t % 2]):
                    W(e, tq)
                    e.wait_ge(vv.h, sqf)
                    e.activation(sq2[:, t % 2], xst[:, buf, k],
                                 AF.Square).then_inc(ss.h, 1)
                bd.em("scalar", sstat)

                vdone = vv.n + 1
                vv.n = vdone
                sqfree[t % 2] = vdone

                def vstat(e, t=t, buf=buf, k=k, tq=tq, sdone=sdone):
                    W(e, tq)
                    tile = xst[:, buf, k]
                    e.reduce_sum(st["negm"][:, t:t + 1], tile, axis=AX.X)
                    e.wait_ge(ss.h, sdone)
                    e.reduce_sum(sqs[:, t:t + 1], sq2[:, t % 2], axis=AX.X)
                    e.reduce_max(st["mx"][:, t:t + 1], tile, axis=AX.X)
                    e.tensor_reduce(st["mn"][:, t:t + 1], tile, axis=AX.X,
                                    op=ALU.min).then_inc(vv.h, 1)
                bd.em("vector", vstat)
            stfree[buf] = vv.n

        base = vv.n
        vv.n = base + 6

        def vvar(e, base=base):
            e.tensor_scalar(st["negm"][:], st["negm"][:], -1.0 / D, None,
                            ALU.mult).then_inc(vv.h, 1)
            e.wait_ge(vv.h, base + 1)
            e.tensor_tensor(st["var"][:], st["negm"][:], st["negm"][:],
                            ALU.mult)
            e.tensor_scalar(st["sc_tmp"][:], sqs[:], 1.0 / D, None,
                            ALU.mult).then_inc(vv.h, 1)
            e.wait_ge(vv.h, base + 2)
            e.tensor_tensor(st["var"][:], st["sc_tmp"][:], st["var"][:],
                            ALU.subtract)
            e.tensor_tensor(st["mx"][:], st["mx"][:], st["negm"][:], ALU.add)
            e.tensor_tensor(st["mn"][:], st["mn"][:], st["negm"][:],
                            ALU.add).then_inc(vv.h, 1)
            e.wait_ge(vv.h, base + 3)
            e.tensor_scalar(st["mn"][:], st["mn"][:], -1.0, None,
                            ALU.mult).then_inc(vv.h, 1)
            e.wait_ge(vv.h, base + 4)
            e.tensor_tensor(st["mx"][:], st["mx"][:], st["mn"][:],
                            ALU.max).then_inc(vv.h, 2)
        bd.em("vector", vvar)
        vneed = vv.n

        ssn = ss.n + 1
        ss.n = ssn

        def srstd(e, vneed=vneed):
            e.wait_ge(vv.h, vneed)
            e.activation(st["rstd"][:], st["var"][:], AF.Sqrt,
                         bias=st["c_eps"][:]).then_inc(ss.h, 1)
        bd.em("scalar", srstd)

        base = vv.n
        vv.n = base + 3

        def vamax(e, ssn=ssn, base=base):
            e.wait_ge(ss.h, ssn)
            e.reciprocal(st["rstd"][:], st["rstd"][:]).then_inc(vv.h, 1)
            e.wait_ge(vv.h, base + 1)
            e.tensor_tensor(st["mx"][:], st["mx"][:], st["rstd"][:],
                            ALU.mult).then_inc(vv.h, 1)
            e.wait_ge(vv.h, base + 2)
            e.reduce_max(st["amax1"][:], st["mx"][:],
                         axis=AX.X).then_inc(vv.h, 1)
        bd.em("vector", vamax)

        qtgt = global_max(lambda: st["amax1"][:], cc0)

        vdone = vv.n + 3
        vv.n = vdone

        def vscale(e, qtgt=qtgt, vdone=vdone):
            W(e, qtgt)
            r = st["scrow"]
            e.tensor_tensor(r[0:1, 1:2], st["sigm"][0:1, 0:1],
                            st["ccres"][0:1, 0:1], ALU.mult).then_inc(vv.h, 1)
            e.wait_ge(vv.h, vdone - 2)
            e.tensor_scalar(r[0:1, 1:2], r[0:1, 1:2], 1.0 / QMAX, None,
                            ALU.mult).then_inc(vv.h, 1)
            e.wait_ge(vv.h, vdone - 1)
            e.reciprocal(r[0:1, 0:1], r[0:1, 1:2]).then_inc(vv.h, 1)
        bd.em("vector", vscale)
        repl_scalars(lambda: st["scrow"][0:1, 0:2], 2, 0, vdone)
        bd.barrier()
        if done():
            if DBG:
                dt1 = qg.tok()
                dt2 = qg.tok()
                dt3 = qg.tok()

                def vdbg(e):
                    e.memset(st["sc_tmp"][:], 7.0).then_inc(vv.h, 1)
                bd.em("vector", vdbg)
                vv.n += 1
                _vd = vv.n

                def gdbg(e, dt1=dt1, dt2=dt2, dt3=dt3, dst=dst, _vd=_vd):
                    e.wait_ge(vv.h, _vd)
                    e.dma_start(out=dst.ap()[6, :, 0:NT],
                                in_=st["sc_tmp"][:]).then_inc(dt1[0].h, 16)
                    e.dma_start(out=dst.ap()[0, :, 0:8],
                                in_=st["sc_bank"][:]).then_inc(dt1[0].h, 16)
                    e.dma_start(out=dst.ap()[0, 0:1, 8:16],
                                in_=st["ccres"][:]).then_inc(dt2[0].h, 16)
                    e.dma_start(out=dst.ap()[0, 0:1, 16:144],
                                in_=st["gath"][:]).then_inc(dt3[0].h, 16)
                    dt4 = None
                    e.dma_start(out=dst.ap()[1, :, 0:NT],
                                in_=st["negm"][:]).then_inc(dt1[0].h, 16)
                    e.dma_start(out=dst.ap()[2, :, 0:NT],
                                in_=st["mx"][:]).then_inc(dt2[0].h, 16)
                    e.dma_start(out=dst.ap()[3, :, 0:D],
                                in_=xst[:, 0, 0]).then_inc(dt3[0].h, 16)
                    e.dma_start(out=dst.ap()[4, 0:1, 0:2],
                                in_=st["sigm"][:]).then_inc(dt1[0].h, 16)
                    e.dma_start(out=xst[:, 0, 0],
                                in_=src.ap()[5]).then_inc(dt2[0].h, 16)
                    e.wait_ge(dt2[0].h, dt2[1] + 32)
                    e.dma_start(out=dst.ap()[5, :, 0:D],
                                in_=xst[:, 0, 0]).then_inc(dt3[0].h, 16)
                    e.wait_ge(dt1[0].h, dt1[1] + 32)
                    e.wait_ge(dt2[0].h, dt2[1] + 32)
                    e.wait_ge(dt3[0].h, dt3[1] + 48)
                bd.em("gpsimd", gdbg)
            return

        base = vv.n
        vv.n = base + 2

        def vcq(e, base=base):
            e.tensor_scalar(st["cq"][:], st["rstd"][:], st["sc_bank"][:, 0:1],
                            None, ALU.mult)
            e.tensor_scalar(st["cg"][:], st["swg"][:], st["sc_bank"][:, 1:2],
                            None, ALU.mult)
            e.tensor_scalar(st["cu"][:], st["swu"][:], st["sc_bank"][:, 1:2],
                            None, ALU.mult).then_inc(vv.h, 1)
            e.wait_ge(vv.h, base + 1)
            e.tensor_tensor(st["m1536"][:], st["negm"][:], st["cq"][:],
                            ALU.mult).then_inc(vv.h, 1)
            e.wait_ge(vv.h, base + 2)
            e.tensor_scalar(st["m1536"][:], st["m1536"][:], R16, None, ALU.add)
        bd.em("vector", vcq)
        bd.barrier()

        bd.phase_reset()
        xq_fm = bd.aa(f"{tag}_xqfm", [128, ND, T], BF16)
        keep = bd.arena - bd.base
        xst2 = bd.aa(f"{tag}_xst2", [128, 2, 2, D], F32)
        xq16 = bd.aa(f"{tag}_xq16", [128, D], F16)
        xqt = bd.aa(f"{tag}_xqt", [128, 2, D], BF16)
        st2free = [0, 0]
        xqtfree = [0, 0]
        psxf = [0, 0]
        for tp in range(NT // 2):
            buf = tp % 2
            tq = qs.tok()

            def dmx2(e, tp=tp, buf=buf, sf=st2free[buf], src=src, tq=tq):
                e.wait_ge(vv.h, sf)
                e.dma_start(out=xst2[:, buf],
                            in_=src.ap()[2 * tp:2 * tp + 2].transpose([1, 0, 2])
                            ).then_inc(tq[0].h, 16)
            bd.em("sync", dmx2)

            for k in range(2):
                t = 2 * tp + k
                vdone = vv.n + 3
                vv.n = vdone

                def vq(e, t=t, buf=buf, k=k, tq=tq, xf=xqtfree[k], vd=vdone):
                    W(e, tq)
                    e.wait_ge(tv.h, xf)
                    tile = xst2[:, buf, k]
                    e.tensor_scalar(xq16[:], tile, st["cq"][:, t:t + 1],
                                    st["m1536"][:, t:t + 1], ALU.mult, ALU.add
                                    ).then_inc(vv.h, 1)
                    e.wait_ge(vv.h, vd - 2)
                    e.tensor_scalar(xq16[:], xq16[:], CLIP_LO, CLIP_HI,
                                    ALU.max, ALU.min).then_inc(vv.h, 1)
                    e.wait_ge(vv.h, vd - 1)
                    e.tensor_scalar(xqt[:, k], xq16[:], -R16, None,
                                    ALU.add).then_inc(vv.h, 1)
                bd.em("vector", vq)

                ttgt = tv.n + 1
                tv.n = ttgt

                def txq(e, t=t, k=k, vdone=vdone, pf=psxf[t % 2]):
                    e.wait_ge(vv.h, max(vdone, pf))
                    inst = None
                    for c in range(ND):
                        inst = e.transpose(
                            ps_b[:, t % 2, c * 128:(c + 1) * 128],
                            xqt[:, k, c * 128:(c + 1) * 128],
                            st["c_idbf"][:])
                    inst.then_inc(tv.h, 1)
                bd.em("tensor", txq)
                xqtfree[k] = ttgt

                vev = vv.n + 1
                vv.n = vev

                def vevx(e, t=t, ttgt=ttgt):
                    e.wait_ge(tv.h, ttgt)
                    e.tensor_copy(
                        xq_fm.ap()[:, :, t * 128:(t + 1) * 128],
                        ps_b[:, t % 2, 0:1024].rearrange("p (c n) -> p c n",
                                                         n=128)).then_inc(vv.h, 1)
                bd.em("vector", vevx)
                psxf[t % 2] = vev
            st2free[buf] = vv.n
        bd.barrier()
        if done():
            return

        bd.phase_reset(keep)
        wgt = bd.aa(f"{tag}_wgt", [128, 2, 2 * ND, 128], BF16)
        silu = bd.aa(f"{tag}_silu", [128, 2, 512], F32)
        g2ev = bd.aa(f"{tag}_g2ev", [128, 2, T], F32)
        wfree = [0, 0]
        g2dfree = [0, 0]
        gupsfree = [0, 0]

        def vg2mx0(e):
            e.memset(st["g2mx"][:], 0.0)
        bd.em("vector", vg2mx0)

        for ic in range(NI):
            wbuf = ic % 2
            tqa = qs.tok()
            tqb = qs.tok()

            def dmw(e, ic=ic, wbuf=wbuf, tf=wfree[wbuf], tqa=tqa, tqb=tqb):
                e.wait_ge(tv.h, tf)
                e.dma_start(out=wgt[:, wbuf, 0:ND],
                            in_=dm["WGT"].ap()[ic]
                            .rearrange("p (c n) -> p c n", n=128)
                            ).then_inc(tqa[0].h, 16)
                e.dma_start(out=wgt[:, wbuf, ND:2 * ND],
                            in_=dm["WUT"].ap()[ic]
                            .rearrange("p (c n) -> p c n", n=128)
                            ).then_inc(tqb[0].h, 16)
            bd.em("sync", dmw)

            for tb in range(8):
                par = tb % 2
                ttgt = tv.n + 1
                tv.n = ttgt

                def tgu(e, ic=ic, tb=tb, par=par, wbuf=wbuf, tqa=tqa,
                        tqb=tqb, pf=gupsfree[par]):
                    W(e, tqa)
                    W(e, tqb)
                    if pf:
                        e.wait_ge(vv.h, pf)
                    for hc in range(ND):
                        e.matmul(ps_f[:, par, :], wgt[:, wbuf, hc],
                                 xq_fm[:, hc, tb * 512:(tb + 1) * 512],
                                 start=(hc == 0), stop=(hc == ND - 1))
                    inst = None
                    for hc in range(ND):
                        inst = e.matmul(
                            ps_f[:, 2 + par, :], wgt[:, wbuf, ND + hc],
                            xq_fm[:, hc, tb * 512:(tb + 1) * 512],
                            start=(hc == 0), stop=(hc == ND - 1))
                    inst.then_inc(tv.h, 1)
                bd.em("tensor", tgu)

                sdone = sv.n + 1
                sv.n = sdone

                def ssilu(e, ic=ic, par=par, ttgt=ttgt):
                    e.wait_ge(tv.h, ttgt)
                    e.activation(silu[:, par], ps_f[:, par, :], AF.Silu,
                                 scale=st["cg"][:, ic:ic + 1]).then_inc(sv.h, 1)
                bd.em("scalar", ssilu)

                vdone = vv.n + 1
                vv.n = vdone

                def vg2(e, ic=ic, tb=tb, par=par, sdone=sdone,
                        gf=g2dfree[ic % 2]):
                    e.wait_ge(sv.h, sdone)
                    if gf != 0:
                        W(e, gf)
                    e.scalar_tensor_tensor(
                        g2ev[:, ic % 2, tb * 512:(tb + 1) * 512],
                        ps_f[:, 2 + par, :], st["cu"][:, ic:ic + 1],
                        silu[:, par], ALU.mult, ALU.mult).then_inc(vv.h, 1)
                bd.em("vector", vg2)
                gupsfree[par] = vdone

            vdone = vv.n + 2
            vv.n = vdone

            def vmax(e, ic=ic, vdone=vdone):
                e.tensor_reduce(st["amax1"][:], g2ev[:, ic % 2, :], axis=AX.X,
                                op=ALU.max,
                                apply_absolute_value=True).then_inc(vv.h, 1)
                e.wait_ge(vv.h, vdone - 1)
                e.tensor_tensor(st["g2mx"][:], st["g2mx"][:], st["amax1"][:],
                                ALU.max).then_inc(vv.h, 1)
            bd.em("vector", vmax)

            tqg = qg.tok()
            vneed = vv.n

            def gg2(e, ic=ic, vneed=vneed, tqg=tqg):
                e.wait_ge(vv.h, vneed)
                e.dma_start(out=dm["G2"].ap()[ic], in_=g2ev[:, ic % 2, :]
                            ).then_inc(tqg[0].h, 16)
            bd.em("gpsimd", gg2)
            g2dfree[ic % 2] = tqg
            wfree[wbuf] = tv.n
        bd.barrier()
        if done():
            return

        qtgt2 = global_max(lambda: st["g2mx"][:], cc0 + 1)
        vdone = vv.n + 4
        vv.n = vdone

        def vscale2(e, qtgt2=qtgt2, vdone=vdone):
            W(e, qtgt2)
            r = st["scrow"]
            e.tensor_tensor(r[0:1, 3:4], st["sigm"][0:1, 1:2],
                            st["ccres"][0:1, 0:1], ALU.mult).then_inc(vv.h, 1)
            e.wait_ge(vv.h, vdone - 3)
            e.tensor_scalar(r[0:1, 3:4], r[0:1, 3:4], 1.0 / QMAX, None,
                            ALU.mult).then_inc(vv.h, 1)
            e.wait_ge(vv.h, vdone - 2)
            e.reciprocal(r[0:1, 2:3], r[0:1, 3:4]).then_inc(vv.h, 1)
            e.wait_ge(vv.h, vdone - 1)
            e.tensor_scalar(r[0:1, 3:4], r[0:1, 3:4], 0.5, None,
                            ALU.mult).then_inc(vv.h, 1)
        bd.em("vector", vscale2)
        repl_scalars(lambda: st["scrow"][0:1, 2:4], 2, 2, vdone)
        bd.barrier()

        def vsdbf(e):
            e.tensor_scalar(st["sdb_f"][:], st["sd_b"][:],
                            st["sc_bank"][:, 3:4], None, ALU.mult)
        bd.em("vector", vsdbf)
        bd.barrier()
        if done():
            return

        bd.phase_reset()
        iwd = bd.aa(f"{tag}_iwd", [128, NI, D], BF16)
        g2st = bd.aa(f"{tag}_g2st", [128, 2, 11, 512], F32)
        h1q = bd.aa(f"{tag}_h1q", [128, NI, 512], BF16)
        h16 = bd.aa(f"{tag}_h16", [128, 11, 512], F16)
        xin = bd.aa(f"{tag}_xin", [128, 4, D], F32)
        xout = bd.aa(f"{tag}_xout", [128, 4, D], F32)
        tmp5 = bd.aa(f"{tag}_tmp5", [128, 512], F32)

        tiwd = qs.tok()

        def dmiwd(e, tiwd=tiwd):
            e.dma_start(out=iwd[:], in_=dm["IWD"].ap()).then_inc(tiwd[0].h, 16)
        bd.em("sync", dmiwd)

        h1qfree = 0
        dpsfree = [0, 0]
        xodfree = 0
        xinfree = 0
        g2stfree = 0

        for ts_ in range(8):
            sb_ = ts_ % 2
            tg0 = qs.tok()
            tg1 = qs.tok()
            txq_ = qs.tok()

            def dmg2(e, ts_=ts_, gf=g2stfree, xf=xinfree, src=src,
                     toks=(tg0, tg1, txq_)):
                e.wait_ge(vv.h, max(gf, xf))
                for half in range(2):
                    e.dma_start(
                        out=g2st[:, half],
                        in_=dm["G2"].ap()[half * 11:(half + 1) * 11, :,
                                          ts_ * 512:(ts_ + 1) * 512]
                        .transpose([1, 0, 2])).then_inc(toks[half][0].h, 16)
                e.dma_start(out=xin[:],
                            in_=src.ap()[ts_ * 4:(ts_ + 1) * 4]
                            .transpose([1, 0, 2])).then_inc(toks[2][0].h, 16)
            bd.em("sync", dmg2)

            vq5 = vv.n + 6
            vv.n = vq5

            def vh(e, ts_=ts_, hqf=h1qfree, toks=(tg0, tg1), vq5=vq5):
                e.wait_ge(tv.h, hqf)
                for half in range(2):
                    b0 = vq5 - 6 + 3 * half
                    W(e, toks[half])
                    g = g2st[:, half].rearrange("p a b -> p (a b)")
                    hh = h16[:].rearrange("p a b -> p (a b)")
                    e.tensor_scalar(hh, g, st["sc_bank"][:, 2:3], R16,
                                    ALU.mult, ALU.add).then_inc(vv.h, 1)
                    e.wait_ge(vv.h, b0 + 1)
                    e.tensor_scalar(hh, hh, CLIP_LO, CLIP_HI, ALU.max,
                                    ALU.min).then_inc(vv.h, 1)
                    e.wait_ge(vv.h, b0 + 2)
                    e.tensor_scalar(
                        h1q[:, half * 11:(half + 1) * 11]
                        .rearrange("p a b -> p (a b)"),
                        hh, -R16, None, ALU.add).then_inc(vv.h, 1)
            bd.em("vector", vh)
            g2stfree = vq5

            for tb4 in range(4):
                par = tb4 % 2
                ttgt = tv.n + 1
                tv.n = ttgt

                def tdown(e, ts_=ts_, tb4=tb4, par=par, vq5=vq5,
                          pf=dpsfree[par], tiwd=tiwd):
                    W(e, tiwd)
                    e.wait_ge(vv.h, max(vq5, pf))
                    inst = None
                    for ic in range(NI):
                        lhs = h1q[:, ic, tb4 * 128:(tb4 + 1) * 128]
                        e.matmul(ps_f[:, 2 * par, :], lhs, iwd[:, ic, 0:512],
                                 start=(ic == 0), stop=(ic == NI - 1))
                        inst = e.matmul(ps_f[:, 2 * par + 1, :], lhs,
                                        iwd[:, ic, 512:1024],
                                        start=(ic == 0), stop=(ic == NI - 1))
                    inst.then_inc(tv.h, 1)
                bd.em("tensor", tdown)

                vdone = vv.n + 3
                vv.n = vdone

                def vdwn(e, tb4=tb4, par=par, ttgt=ttgt, txq_=txq_,
                         gf=xodfree, vd=vdone):
                    e.wait_ge(tv.h, ttgt)
                    W(e, txq_)
                    if gf != 0:
                        W(e, gf)
                    e.tensor_tensor(tmp5[:], ps_f[:, 2 * par, :],
                                    st["sdb_f"][:, 0:512],
                                    ALU.mult).then_inc(vv.h, 1)
                    e.wait_ge(vv.h, vd - 2)
                    e.tensor_tensor(xout[:, tb4, 0:512], tmp5[:],
                                    xin[:, tb4, 0:512], ALU.add)
                    e.tensor_tensor(tmp5[:], ps_f[:, 2 * par + 1, :],
                                    st["sdb_f"][:, 512:1024],
                                    ALU.mult).then_inc(vv.h, 1)
                    e.wait_ge(vv.h, vd - 1)
                    e.tensor_tensor(xout[:, tb4, 512:1024], tmp5[:],
                                    xin[:, tb4, 512:1024],
                                    ALU.add).then_inc(vv.h, 1)
                bd.em("vector", vdwn)
                dpsfree[par] = vdone

            h1qfree = tv.n
            xinfree = vv.n
            tqg = qg.tok()
            vneed = vv.n

            def gxout(e, ts_=ts_, vneed=vneed, dst=dst, tqg=tqg):
                e.wait_ge(vv.h, vneed)
                e.dma_start(out=dst.ap()[ts_ * 4:(ts_ + 1) * 4]
                            .transpose([1, 0, 2]),
                            in_=xout[:]).then_inc(tqg[0].h, 16)
            bd.em("gpsimd", gxout)
            xodfree = tqg
        bd.barrier()


    def attn(tag, src, dst):
        bd.phase_reset()
        z_fm = bd.aa(f"{tag}_zfm", [128, ND, S], F32R)
        keepz = bd.arena - bd.base
        za = bd.aa(f"{tag}_za", [128, 2, D], F32)
        sqa = bd.aa(f"{tag}_sqa", [128, 2, D], F32)
        zt = bd.aa(f"{tag}_zt", [128, 2, D], F32R)
        s1a = bd.aa(f"{tag}_s1a", [128, 8], F32)
        qraw = bd.aa(f"{tag}_qraw", [128, S], F32)
        qsw = bd.aa(f"{tag}_qsw", [128, S], F32)
        qr = bd.aa(f"{tag}_qr", [128, S], F32R)
        kr = bd.aa(f"{tag}_kr", [128, S], F32R)
        wqp = bd.aa(f"{tag}_wqp", [128, 3, ND, 128], F32R)
        vfm = bd.aa(f"{tag}_vfm", [128, S], F32R)
        vtok = bd.aa(f"{tag}_vtok", [128, TPB * 130], F32R)
        pt = bd.aa(f"{tag}_pt", [128, 2, 512], F32R)
        recipb = bd.aa(f"{tag}_recipb", [128, 512], F32)
        aoh = bd.aa(f"{tag}_aoh", [128, S], F32R)
        projfree = [0, 0]
        projfree2 = [0, 0]
        accfree = [None]
        oxoutfree = [0, 0]
        pvfree = [0]

        for b in range(BC):
            tb0 = b * TPB
            zafree = [0, 0]
            zpsfree = [0, 0]
            for t_ in range(TPB):
                buf = t_ % 2
                tqz = qs.tok()

                def dmz(e, t_=t_, buf=buf, zf=zafree[buf], tqz=tqz, tb0=tb0,
                        src=src):
                    e.wait_ge(vv.h, zf)
                    e.dma_start(out=za[:, buf], in_=src.ap()[tb0 + t_]
                                ).then_inc(tqz[0].h, 16)
                bd.em("sync", dmz)

                sq_d = ss.n + 1
                ss.n = sq_d

                def sstz(e, buf=buf, tqz=tqz):
                    W(e, tqz)
                    e.activation(sqa[:, buf], za[:, buf],
                                 AF.Square).then_inc(ss.h, 1)
                bd.em("scalar", sstz)

                vz = vv.n + 5
                vv.n = vz
                s2_d = ss.n + 1
                ss.n = s2_d

                def vstz(e, buf=buf, tqz=tqz, vz=vz, sq_d=sq_d):
                    W(e, tqz)
                    e.reduce_sum(s1a[:, 0:1], za[:, buf],
                                 axis=AX.X).then_inc(vv.h, 1)
                    e.wait_ge(ss.h, sq_d)
                    e.reduce_sum(s1a[:, 1:2], sqa[:, buf],
                                 axis=AX.X).then_inc(vv.h, 1)
                    e.wait_ge(vv.h, vz - 3)
                    e.tensor_scalar(s1a[:, 2:3], s1a[:, 0:1], -1.0 / D, None,
                                    ALU.mult)
                    e.tensor_scalar(s1a[:, 3:4], s1a[:, 1:2], 1.0 / D, None,
                                    ALU.mult).then_inc(vv.h, 1)
                    e.wait_ge(vv.h, vz - 2)
                    e.tensor_tensor(s1a[:, 4:5], s1a[:, 2:3], s1a[:, 2:3],
                                    ALU.mult).then_inc(vv.h, 1)
                    e.wait_ge(vv.h, vz - 1)
                    e.tensor_tensor(s1a[:, 3:4], s1a[:, 3:4], s1a[:, 4:5],
                                    ALU.subtract).then_inc(vv.h, 1)
                bd.em("vector", vstz)

                def srs(e, vz=vz):
                    e.wait_ge(vv.h, vz)
                    e.activation(s1a[:, 5:6], s1a[:, 3:4], AF.Sqrt,
                                 bias=st["c_eps"][:]).then_inc(ss.h, 1)
                bd.em("scalar", srs)

                vz2 = vv.n + 3
                vv.n = vz2

                def vnz(e, buf=buf, vz2=vz2, s2_d=s2_d):
                    e.wait_ge(ss.h, s2_d)
                    e.reciprocal(s1a[:, 6:7], s1a[:, 5:6]).then_inc(vv.h, 1)
                    e.wait_ge(vv.h, vz2 - 2)
                    e.tensor_tensor(s1a[:, 7:8], s1a[:, 2:3], s1a[:, 6:7],
                                    ALU.mult).then_inc(vv.h, 1)
                    e.wait_ge(vv.h, vz2 - 1)
                    e.tensor_scalar(zt[:, buf], za[:, buf], s1a[:, 6:7],
                                    s1a[:, 7:8], ALU.mult,
                                    ALU.add).then_inc(vv.h, 1)
                bd.em("vector", vnz)

                ttz = tv.n + 1
                tv.n = ttz

                def tz(e, t_=t_, buf=buf, vz2=vz2, pf=zpsfree[t_ % 2]):
                    e.wait_ge(vv.h, max(vz2, pf))
                    psz = ps_f[:, 2 * (t_ % 2):2 * (t_ % 2) + 2, :]\
                        .rearrange("p a b -> p (a b)").bitcast(F32R)
                    inst = None
                    for c in range(ND):
                        inst = e.transpose(
                            psz[:, c * 128:(c + 1) * 128],
                            zt[:, buf, c * 128:(c + 1) * 128], st["c_idr"][:])
                    inst.then_inc(tv.h, 1)
                bd.em("tensor", tz)

                vz3 = vv.n + 1
                vv.n = vz3

                def vez(e, t_=t_, ttz=ttz):
                    e.wait_ge(tv.h, ttz)
                    psz = ps_f[:, 2 * (t_ % 2):2 * (t_ % 2) + 2, :]\
                        .rearrange("p a b -> p (a b)").bitcast(F32R)
                    e.tensor_copy(
                        z_fm.ap()[:, :, t_ * 128:(t_ + 1) * 128],
                        psz.rearrange("p (c n) -> p c n", n=128)
                        ).then_inc(vv.h, 1)
                bd.em("vector", vez)
                zpsfree[t_ % 2] = vz3
                zafree[buf] = vz3
            bd.barrier()

            for p in range(NPAIR):
                if p > 0:
                    bd.barrier()
                twp3 = [qs.tok() for _ in range(3)]

                def dmwp(e, p=p, toks=tuple(twp3)):
                    for i, wnm in enumerate(("wq", "wk", "wv")):
                        e.dma_start(out=wqp[:, i],
                                    in_=dm[wnm].ap()[p].transpose([1, 0, 2])
                                    ).then_inc(toks[i][0].h, 16)
                bd.em("sync", dmwp)

                for which, outb in ((0, qr), (1, kr)):
                    for qt4 in range(4):
                        par = qt4 % 2
                        tpq = tv.n + 1
                        tv.n = tpq

                        def tproj(e, which=which, qt4=qt4, par=par,
                                  twp=twp3[which], pf=projfree[par]):
                            W(e, twp)
                            if pf:
                                e.wait_ge(vv.h, pf)
                            inst = None
                            for dc in range(ND):
                                inst = e.matmul(
                                    ps_f[:, par, :], wqp[:, which, dc],
                                    z_fm[:, dc, qt4 * 512:(qt4 + 1) * 512],
                                    start=(dc == 0), stop=(dc == ND - 1))
                            inst.then_inc(tv.h, 1)
                        bd.em("tensor", tproj)

                        vpq = vv.n + 1
                        vv.n = vpq

                        def vproj(e, qt4=qt4, par=par, tpq=tpq):
                            e.wait_ge(tv.h, tpq)
                            e.tensor_copy(qraw[:, qt4 * 512:(qt4 + 1) * 512],
                                          ps_f[:, par, :]).then_inc(vv.h, 1)
                        bd.em("vector", vproj)
                        projfree[par] = vpq

                    tsw4 = [qs.tok() for _ in range(4)]
                    vneed_q = vv.n

                    def dmsw(e, toks=tuple(tsw4), vneed_q=vneed_q):
                        e.wait_ge(vv.h, vneed_q)
                        for blk in range(4):
                            sp = [32, 0, 96, 64][blk]
                            e.dma_start(
                                out=qsw[blk * 32:(blk + 1) * 32, :],
                                in_=qraw[sp:sp + 32, :]
                                ).then_inc(toks[blk][0].h, 16)
                    bd.em("sync", dmsw)

                    vr = vv.n + 2
                    vv.n = vr

                    def vrope(e, outb=outb, toks=tuple(tsw4), vr=vr):
                        for tok in toks:
                            W(e, tok)
                        e.tensor_tensor(qraw[:], qraw[:], st["c_t1"][:],
                                        ALU.mult)
                        e.tensor_tensor(qsw[:], qsw[:], st["c_t2"][:],
                                        ALU.mult).then_inc(vv.h, 1)
                        e.wait_ge(vv.h, vr - 1)
                        e.tensor_tensor(outb[:], qraw[:], qsw[:],
                                        ALU.add).then_inc(vv.h, 1)
                    bd.em("vector", vrope)
                ropedone = vv.n

                for vt4 in range(4):
                    par = vt4 % 2
                    tpv = tv.n + 1
                    tv.n = tpv

                    def tprojv(e, vt4=vt4, par=par, twp=twp3[2],
                               pf=projfree[par]):
                        W(e, twp)
                        if pf:
                            e.wait_ge(vv.h, pf)
                        inst = None
                        for dc in range(ND):
                            inst = e.matmul(
                                ps_f[:, par, :], wqp[:, 2, dc],
                                z_fm[:, dc, vt4 * 512:(vt4 + 1) * 512],
                                start=(dc == 0), stop=(dc == ND - 1))
                        inst.then_inc(tv.h, 1)
                    bd.em("tensor", tprojv)

                    vpv = vv.n + 1
                    vv.n = vpv

                    def vprojv(e, vt4=vt4, par=par, tpv=tpv):
                        e.wait_ge(tv.h, tpv)
                        e.tensor_copy(vfm[:, vt4 * 512:(vt4 + 1) * 512],
                                      ps_f[:, par, :]).then_inc(vv.h, 1)
                    bd.em("vector", vprojv)
                    projfree[par] = vpv
                vfmdone = vv.n

                vo = vv.n + 1
                vv.n = vo

                def vones(e, vo=vo):
                    src3 = bass.AP(st["c_onesr"], 0,
                                   [[16, 128], [1, TPB], [1, 1]])
                    e.tensor_copy(
                        bass.AP(vtok, 64,
                                [[TPB * 130, 128], [130, TPB], [1, 1]]),
                        src3)
                    e.tensor_copy(
                        bass.AP(vtok, 129,
                                [[TPB * 130, 128], [130, TPB], [1, 1]]),
                        src3).then_inc(vv.h, 1)
                bd.em("vector", vones)

                for kc2 in range(TPB):
                    par = kc2 % 2
                    tvt_ = tv.n + 1
                    tv.n = tvt_

                    def tvtr(e, kc2=kc2, par=par, vfmdone=vfmdone,
                             pf=projfree2[kc2 % 2]):
                        e.wait_ge(vv.h, max(vfmdone, pf))
                        e.transpose(ps_f[:, 2 + par, 0:128].bitcast(F32R),
                                    vfm[:, kc2 * 128:(kc2 + 1) * 128],
                                    st["c_idr"][:]).then_inc(tv.h, 1)
                    bd.em("tensor", tvtr)

                    vvt = vv.n + 1
                    vv.n = vvt

                    def vevt(e, kc2=kc2, par=par, tvt_=tvt_):
                        e.wait_ge(tv.h, tvt_)
                        e.tensor_copy(
                            bass.AP(vtok, kc2 * 130,
                                    [[TPB * 130, 128], [65, 2], [1, 64]]),
                            ps_f[:, 2 + par, 0:128].bitcast(F32R)
                            .rearrange("p (a n) -> p a n", n=64)
                            ).then_inc(vv.h, 1)
                    bd.em("vector", vevt)
                    projfree2[kc2 % 2] = vvt
                vtokdone = vv.n

                for hh in range(2):
                    hbase = hh * 64
                    head = p * 2 + hh
                    for w in range(4):
                        nkc = 4 * w + 4
                        exp_t = {}
                        pv_t = {}
                        last_pv = None
                        pvfree_w = pvfree[0]
                        for kc in range(nkc):
                            diag = kc >= 4 * w
                            tS = tv.n + 1
                            tv.n = tS

                            def tqk(e, kc=kc, w=w, hbase=hbase, diag=diag,
                                    ef=exp_t.get(kc - 2, 0),
                                    rd=max(ropedone, vtokdone)):
                                e.wait_ge(vv.h, rd)
                                if ef:
                                    e.wait_ge(sv.h, ef)
                                inst = e.matmul(
                                    ps_f[:, kc % 2, :],
                                    kr[hbase:hbase + 64,
                                       kc * 128:(kc + 1) * 128],
                                    qr[hbase:hbase + 64,
                                       w * 512:(w + 1) * 512],
                                    start=True, stop=not diag)
                                if diag:
                                    off = 384 - 128 * (kc - 4 * w)
                                    inst = e.matmul(
                                        ps_f[:, kc % 2, :], st["c_idr"][:],
                                        st["c_tri"][:, off:off + 512],
                                        start=False, stop=True)
                                inst.then_inc(tv.h, 1)
                            bd.em("tensor", tqk)

                            sE = sv.n + 1
                            sv.n = sE
                            exp_t[kc] = sE

                            def sexp(e, kc=kc, tS=tS, pvf=pv_t.get(kc - 2, 0)):
                                e.wait_ge(tv.h, tS)
                                if pvf:
                                    e.wait_ge(tv.h, pvf)
                                e.activation(pt[:, kc % 2], ps_f[:, kc % 2, :],
                                             AF.Exp, bias=st["c_neg10"][:],
                                             scale=0.125).then_inc(sv.h, 1)
                            bd.em("scalar", sexp)

                            tP = tv.n + 1
                            tv.n = tP
                            pv_t[kc] = tP

                            def tpv(e, kc=kc, hh=hh, w=w, nkc=nkc, sE=sE,
                                    pvf=pvfree[0] if kc == 0 else 0):
                                e.wait_ge(sv.h, sE)
                                if kc == 0 and pvf:
                                    e.wait_ge(vv.h, pvf)
                                e.matmul(
                                    ps_f[0:65, 4, :],
                                    vtok[:, kc * 130 + hh * 65:
                                         kc * 130 + hh * 65 + 65],
                                    pt[:, kc % 2],
                                    start=(kc == 0),
                                    stop=(kc == nkc - 1)).then_inc(tv.h, 1)
                            bd.em("tensor", tpv)
                            last_pv = tP

                        vD = vv.n + 1
                        vv.n = vD

                        def vrec(e, last_pv=last_pv):
                            e.wait_ge(tv.h, last_pv)
                            e.reciprocal(recipb[64:65, :],
                                         ps_f[64:65, 4, :]).then_inc(vv.h, 1)
                        bd.em("vector", vrec)

                        tR = tv.n + 1
                        tv.n = tR

                        def trep(e, vD=vD):
                            e.wait_ge(vv.h, vD)
                            e.matmul(ps_f[0:64, 3, :],
                                     st["onesall"][64:65, 0:64],
                                     recipb[64:65, :], start=True,
                                     stop=True).then_inc(tv.h, 1)
                        bd.em("tensor", trep)

                        vN = vv.n + 2
                        vv.n = vN

                        def vnorm(e, w=w, tR=tR, vN=vN):
                            e.wait_ge(tv.h, tR)
                            e.tensor_copy(recipb[0:64, :],
                                          ps_f[0:64, 3, :]).then_inc(vv.h, 1)
                            e.wait_ge(vv.h, vN - 1)
                            e.tensor_tensor(
                                aoh[0:64, w * 512:(w + 1) * 512],
                                ps_f[0:64, 4, :], recipb[0:64, :],
                                ALU.mult).then_inc(vv.h, 1)
                        bd.em("vector", vnorm)
                        pvfree[0] = vN

                    tA = qg.tok()
                    vneed_h = vv.n

                    def gacc(e, head=head, tA=tA, vneed_h=vneed_h):
                        e.wait_ge(vv.h, vneed_h)
                        e.dma_start(
                            out=dm["ACC"].ap()[head // 2,
                                               (head % 2) * 64:
                                               (head % 2) * 64 + 64, :],
                            in_=aoh[0:64, :]).then_inc(tA[0].h, 16)
                    bd.em("gpsimd", gacc)
                    accfree[0] = tA
            bd.barrier()

            bd.phase_reset()
            acc_sb = bd.aa(f"{tag}_acc{b}", [128, ND, S], F32R)
            wo_sb = bd.aa(f"{tag}_wo{b}", [128, ND, D], F32R)
            wob_b = bd.aa(f"{tag}_wob{b}", [128, D], F32)
            xio = bd.aa(f"{tag}_xio{b}", [128, 2, D], F32)
            xoo = bd.aa(f"{tag}_xoo{b}", [128, 2, D], F32)
            tmpo = bd.aa(f"{tag}_tmpo{b}", [128, D], F32)

            tacc = qs.tok()
            two = qs.tok()
            twb = qs.tok()

            def dmoa(e, tacc=tacc, two=two, twb=twb):
                e.dma_start(out=acc_sb[:],
                            in_=dm["ACC"].ap().transpose([1, 0, 2])
                            ).then_inc(tacc[0].h, 16)
                e.dma_start(out=wo_sb[:],
                            in_=dm["wo"].ap().transpose([1, 0, 2])
                            ).then_inc(two[0].h, 16)
                e.dma_start(out=wob_b[:],
                            in_=bass.AP(dm["wob"], 0, [[0, 128], [1, D]])
                            ).then_inc(twb[0].h, 16)
            bd.em("sync", dmoa)

            ofree = [0, 0]
            oxfree = [0, 0]
            for tb in range(TPB):
                par = tb % 2
                txo = qs.tok()

                def dmxo(e, tb=tb, par=par, tb0=tb0, txo=txo,
                         xf=oxfree[par], src=src):
                    if xf:
                        e.wait_ge(vv.h, xf)
                    e.dma_start(out=xio[:, par], in_=src.ap()[tb0 + tb]
                                ).then_inc(txo[0].h, 16)
                bd.em("sync", dmxo)

                tO = tv.n
                tv.n = tO + 1

                def toproj2(e, tb=tb, par=par, tacc=tacc, two=two,
                            pf=ofree[par]):
                    W(e, tacc)
                    W(e, two)
                    if pf:
                        e.wait_ge(vv.h, pf)
                    inst = None
                    for fc in range(ND):
                        e.matmul(ps_f[:, par, :],
                                 acc_sb[:, fc, tb * 128:(tb + 1) * 128],
                                 wo_sb[:, fc, 0:512],
                                 start=(fc == 0), stop=(fc == ND - 1))
                        inst = e.matmul(ps_f[:, 2 + par, :],
                                        acc_sb[:, fc, tb * 128:(tb + 1) * 128],
                                        wo_sb[:, fc, 512:1024],
                                        start=(fc == 0), stop=(fc == ND - 1))
                    inst.then_inc(tv.h, 1)
                bd.em("tensor", toproj2)

                vO = vv.n + 4
                vv.n = vO

                def voproj(e, tb=tb, par=par, tO=tO, txo=txo, twb=twb,
                           gf=oxoutfree[par], vO=vO):
                    e.wait_ge(tv.h, tO + 1)
                    W(e, txo)
                    W(e, twb)
                    if gf:
                        W(e, gf)
                    for half in range(2):
                        sl = slice(half * 512, (half + 1) * 512)
                        bank = par if half == 0 else 2 + par
                        e.tensor_tensor(tmpo[:, sl], ps_f[:, bank, :],
                                        wob_b[:, sl],
                                        ALU.add).then_inc(vv.h, 1)
                        e.wait_ge(vv.h, vO - 3 + 2 * half)
                        e.tensor_tensor(xoo[:, par, sl], tmpo[:, sl],
                                        xio[:, par, sl],
                                        ALU.add).then_inc(vv.h, 1)
                bd.em("vector", voproj)
                ofree[par] = vO
                oxfree[par] = vO

                tgo = qg.tok()
                vneed_o = vv.n

                def gxo(e, tb=tb, par=par, tb0=tb0, vneed_o=vneed_o, tgo=tgo,
                        dst=dst):
                    e.wait_ge(vv.h, vneed_o)
                    e.dma_start(out=dst.ap()[tb0 + tb], in_=xoo[:, par]
                                ).then_inc(tgo[0].h, 16)
                bd.em("gpsimd", gxo)
                oxoutfree[par] = tgo
            bd.barrier()
            bd.phase_reset(keepz)

    prologue()
    if STAGE == 0:
        pass
    elif STAGE == 1:
        ffn("f1", dm["x"], dm["out"], 0)
    elif STAGE == 2:
        ffn("f1", dm["x"], dm["X"], 0)
        attn("a1", dm["X"], dm["out"])
    else:
        ffn("f1", dm["x"], dm["X"], 0)
        attn("a1", dm["X"], dm["X"])
        ffn("f2", dm["X"], dm["out"], 2)

    if DBG == 3:
        toks = [qg.tok() for _ in range(6)]

        def gdbg3(e, toks=tuple(toks)):
            dbg = dm["dbgout"].ap()
            e.dma_start(out=dbg[:, 0:8], in_=st["sc_bank"][:]
                        ).then_inc(toks[0][0].h, 16)
            e.dma_start(out=dbg[0:1, 8:16], in_=st["ccres"][:]
                        ).then_inc(toks[1][0].h, 16)
            e.dma_start(out=dbg[:, 16:48], in_=st["negm"][:]
                        ).then_inc(toks[2][0].h, 16)
            e.dma_start(out=dbg[:, 48:80], in_=st["rstd"][:]
                        ).then_inc(toks[3][0].h, 16)
            e.dma_start(out=dbg[:, 80:1104], in_=dm["G2"].ap()[0, :, 0:1024]
                        ).then_inc(toks[4][0].h, 16)
            e.dma_start(out=dbg[:, 1104:1616].bitcast(BF16),
                        in_=dm["WGT"].ap()[0, :, :]
                        ).then_inc(toks[5][0].h, 16)
            for tk in toks:
                W(e, tk)
        bd.em("gpsimd", gdbg3)

    gsnap = tuple(qg.snapshot())
    ssnap = tuple(qs.snapshot())

    def tail_g(e, snap=gsnap):
        for s, v in snap:
            e.wait_ge(s.h, v)
    bd.em("gpsimd", tail_g)

    def tail_s(e, snap=ssnap):
        for s, v in snap:
            e.wait_ge(s.h, v)
    bd.em("sync", tail_s)

    with nc.Block() as block:
        @block.sync
        def _(e):
            for f in bd.code["sync"]:
                f(e)

        @block.vector
        def _(e):
            for f in bd.code["vector"]:
                f(e)

        @block.scalar
        def _(e):
            for f in bd.code["scalar"]:
                f(e)

        @block.tensor
        def _(e):
            for f in bd.code["tensor"]:
                f(e)

        @block.gpsimd
        def _(e):
            for f in bd.code["gpsimd"]:
                f(e)

    es.close()
    return nc


def _host_consts():
    inv = 1.0 / (10000.0 ** (np.arange(0, HD, 2)[: HD // 2].astype(np.float64)
                             / HD))
    ang = np.outer(np.arange(S, dtype=np.float64), inv)
    cosv = np.cos(ang).astype(np.float32)
    sinv = np.sin(ang).astype(np.float32)
    t1 = np.zeros((128, S), np.float32)
    t2 = np.zeros((128, S), np.float32)
    for blk in range(4):
        rows = slice(blk * 32, blk * 32 + 32)
        t1[rows] = cosv.T
        t2[rows] = sinv.T if blk % 2 == 1 else -sinv.T
    idbf = np.eye(128, dtype=ml_dtypes.bfloat16)
    idr = np.eye(128, dtype=np.float32)
    cc = np.arange(1024)[None, :]
    kk = np.arange(128)[:, None]
    tri = np.where(cc < kk + 384, np.float32(NEGBIG),
                   np.float32(0.0)).astype(np.float32)
    return t1, t2, idbf, idr, tri


def _perm_cols():
    perm = []
    for h in range(H):
        perm += [h * HD + i for i in range(0, HD, 2)]
        perm += [h * HD + i for i in range(1, HD, 2)]
    return np.array(perm)


def kernel(L0_attn_x, L0_attn_wq, L0_attn_wk, L0_attn_wv, L0_attn_wo_w,
           L0_attn_wo_b, L0_ffn_clip_x, L0_ffn_clip_down, L0_ffn_w_gate,
           L0_ffn_w_up, L0_ffn_w_down):
    x = np.ascontiguousarray(np.asarray(L0_attn_x, dtype=np.float32))
    wq = np.asarray(L0_attn_wq, np.float32)
    wk = np.asarray(L0_attn_wk, np.float32)
    wv = np.asarray(L0_attn_wv, np.float32)
    wo = np.asarray(L0_attn_wo_w, np.float32)
    wob = np.asarray(L0_attn_wo_b, np.float32).reshape(1, D)
    wg = np.ascontiguousarray(np.asarray(L0_ffn_w_gate, np.float32))
    wu = np.ascontiguousarray(np.asarray(L0_ffn_w_up, np.float32))
    wd = np.ascontiguousarray(np.asarray(L0_ffn_w_down, np.float32))
    clip = np.array([[float(np.asarray(L0_ffn_clip_x).reshape(-1)[0]),
                      float(np.asarray(L0_ffn_clip_down).reshape(-1)[0])]],
                    np.float32)

    perm = _perm_cols()
    wq_p, wk_p = wq[:, perm], wk[:, perm]

    def pair_tile(w):
        return np.ascontiguousarray(
            w.reshape(ND, 128, NPAIR, 128).transpose(2, 0, 1, 3))

    wq_t, wk_t, wv_t = pair_tile(wq_p), pair_tile(wk_p), pair_tile(wv)
    wo_t = np.ascontiguousarray(wo.reshape(ND, 128, D))
    t1, t2, idbf, idr, tri = _host_consts()

    nc = build_program()
    common = dict(wg=wg, wu=wu, wd=wd, clip=clip, idbf=idbf)
    if STAGE >= 2:
        common.update(wq=wq_t, wk=wk_t, wv=wv_t, wo=wo_t, wob=wob, t1=t1,
                      t2=t2, idr=idr, trimask=tri)
    in_maps = []
    for c in range(NCORE):
        xc = x[c * BC:(c + 1) * BC].reshape(NT, 128, D)
        in_maps.append(dict(common, x=np.ascontiguousarray(xc)))
    res = run_bass_kernel_spmd(nc, in_maps, core_ids=list(range(NCORE)))
    global LAST_RESULTS
    LAST_RESULTS = res
    outs = [r["out"].reshape(BC, S, D) for r in res.results]
    return np.concatenate(outs, axis=0)


# version 02
# speedup vs baseline: 1.0098x; 1.0098x over previous
import contextlib
import numpy as np
import ml_dtypes
import concourse.bass as bass
import concourse.mybir as mybir
from concourse.bass_utils import run_bass_kernel_spmd

F32 = mybir.dt.float32
F32R = mybir.dt.float32r
BF16 = mybir.dt.bfloat16
F16 = mybir.dt.float16
AF = mybir.ActivationFunctionType
ALU = mybir.AluOpType
AX = mybir.AxisListType

B, S, D = 16, 2048, 1024
NCORE = 8
BC = B // NCORE
T = BC * S
NT = T // 128
TPB = S // 128
II = 2816
NI = II // 128
ND = D // 128
H, HD = 16, 64
NPAIR = H // 2
EPS = 1e-5
QMAX = 127.0
R16 = 1536.0
CLIP_LO, CLIP_HI = R16 - 128.0, R16 + 127.0
NEGBIG = -1.0e9
EXPBIAS = -10.0

STAGE = 3
UPTO = 99
PUPTO = 99
NWT = 99
QSTEPS = 'LQTES'
DBG = 0
LAST_RESULTS = None
DELAY = 0

SBUF_LIMIT = 229344
ARENA_BASE = 56 * 1024


class Sem:
    __slots__ = ("name", "n", "h")

    def __init__(self, name):
        self.name, self.n, self.h = name, 0, None


class Queue:

    def __init__(self, name, nsems):
        self.name = name
        self.subs = [Sem(f"{name}{j}") for j in range(nsems)]
        self.idx = 0

    def tok(self):
        j = self.idx % len(self.subs)
        self.idx += 1
        s = self.subs[j]
        s.n += 16
        return (s, s.n)

    def snapshot(self):
        return [(s, s.n) for s in self.subs if s.n > 0]


class Builder:
    ENGINES = ("sync", "vector", "scalar", "tensor", "gpsimd")

    def __init__(self, nc):
        self.nc = nc
        self.code = {e: [] for e in self.ENGINES}
        self.sems = {}
        self.nbar = 0
        self.base = ARENA_BASE
        self.arena = ARENA_BASE

    def sem(self, name):
        s = Sem(name)
        self.sems[name] = s
        return s

    def em(self, eng, fn):
        self.code[eng].append(fn)

    def set_base(self, b):
        self.base = (b + 31) & ~31
        self.arena = self.base

    def phase_reset(self, keep=0):
        self.arena = self.base + keep

    def aa(self, name, shape, dtype):
        sz = int(np.prod(shape[1:])) * mybir.dt.size(dtype)
        off = (self.arena + 31) & ~31
        self.arena = off + sz
        assert self.arena <= SBUF_LIMIT, (name, self.arena)
        if not hasattr(self, "offs"):
            self.offs = {}
        self.offs[name] = off
        return self.nc.alloc_sbuf_tensor_at(name, shape, dtype, offset=off)

    def alias(self, name, shape, dtype, other):
        sz = int(np.prod(shape[1:])) * mybir.dt.size(dtype)
        off = self.offs[other]
        assert off + sz <= SBUF_LIMIT, (name, off + sz)
        return self.nc.alloc_sbuf_tensor_at(name, shape, dtype, offset=off)

    def barrier(self):
        self.nbar += 1
        tgt = 5 * self.nbar
        snap = []
        for q in self.queues:
            snap += q.snapshot()
        bar = self.sems["bar"]
        first = self.nbar == 1
        for eng in self.ENGINES:
            def f(e, t=tgt, snap=tuple(snap), eng=eng, first=first):
                if eng == "tensor" and not first:
                    e.transpose(self.dummy_ps, self.dummy_id,
                                self.dummy_id).then_inc(bar.h, 1)
                elif eng == "vector":
                    e.memset(self.dmy_v, 0.0).then_inc(bar.h, 1)
                elif eng == "scalar":
                    e.memzero(self.dmy_s).then_inc(bar.h, 1)
                elif eng == "gpsimd":
                    e.memset(self.dmy_g, 0.0).then_inc(bar.h, 1)
                else:
                    e.nop().then_inc(bar.h, 1)
                e.wait_ge(bar.h, t)
                for s, v in snap:
                    e.wait_ge(s.h, v)
            self.em(eng, f)
        bar.n = tgt


def build_program():
    nc = bass.Bass("TRN2", target_bir_lowering=False,
                   detect_race_conditions=False)
    bd = Builder(nc)

    dm = {}

    def dr(name, shape, dtype, kind="Internal"):
        dm[name] = nc.dram_tensor(name, shape, dtype, kind=kind)
        return dm[name]

    dr("x", [NT, 128, D], F32, "ExternalInput")
    dr("wg", [II, D], F32, "ExternalInput")
    dr("wu", [II, D], F32, "ExternalInput")
    dr("wd", [D, II], F32, "ExternalInput")
    dr("clip", [1, 2], F32, "ExternalInput")
    if STAGE >= 2:
        dr("wq", [NPAIR, ND, 128, 128], F32R, "ExternalInput")
        dr("wk", [NPAIR, ND, 128, 128], F32R, "ExternalInput")
        dr("wv", [NPAIR, ND, 128, 128], F32R, "ExternalInput")
        dr("wo", [ND, 128, D], F32R, "ExternalInput")
        dr("wob", [1, D], F32, "ExternalInput")
        dr("t1", [128, S], F32, "ExternalInput")
        dr("t2", [128, S], F32, "ExternalInput")
        dr("idr", [128, 128], F32R, "ExternalInput")
        dr("trimask", [128, 1024], F32R, "ExternalInput")
    dr("idbf", [128, 128], BF16, "ExternalInput")
    dr("out", [NT, 128, D], F32, "ExternalOutput")
    if DBG == 3:
        dr("dbgout", [128, 4096], F32, "ExternalOutput")
    dr("X", [NT, 128, D], F32)
    dr("G2", [NI, 128, T], F32)
    dr("WGT", [NI, 128, ND * 128], BF16)
    dr("WUT", [NI, 128, ND * 128], BF16)
    dr("IWD", [128, NI, D], BF16)
    dr("ACC", [ND, 128, S], F32R)
    dr("gbounce", [1, 128], F32)
    dr("rowbounce", [1, D], F32)
    dr("scbounce", [1, 8], F32)
    dr("sdbounce", [128, ND], F32)
    for i in range(4):
        dr(f"ccin{i}", [1, 8], F32)
        dr(f"ccout{i}", [1, 8], F32)

    es = contextlib.ExitStack()
    qs = Queue("qs", 4)
    qg = Queue("qg", 4)
    qc = Queue("qc", 1)
    bd.queues = [qs, qg, qc]
    for q in bd.queues:
        for s in q.subs:
            s.h = es.enter_context(nc.semaphore(s.name))
    for name in ("tv", "vt", "sv", "vv", "ss", "gg", "bar"):
        s = bd.sem(name)
        s.h = es.enter_context(nc.semaphore(name))
    SM = bd.sems
    tv, vt, sv, vv, ss, gg = (SM["tv"], SM["vt"], SM["sv"], SM["vv"],
                              SM["ss"], SM["gg"])

    def W(e, tok):
        e.wait_ge(tok[0].h, tok[1])

    st = {}

    def sb(name, shape, dtype):
        st[name] = nc.alloc_sbuf_tensor(name, shape, dtype)
        return st[name]

    sb("c_t1", [128, S], F32)
    sb("c_t2", [128, S], F32)
    sb("c_idbf", [128, 128], BF16)
    sb("c_idr", [128, 128], F32R)
    sb("c_tri", [128, 1024], F32R)
    sb("c_eps", [128, 1], F32)
    sb("c_neg10", [128, 1], F32)
    sb("c_ones", [1, 128], F32)
    sb("c_onesr", [128, 16], F32R)
    sb("c_ones16", [128, 16], F32)
    sb("sigm", [1, 2], F32)
    sb("negm", [128, NT], F32)
    sb("var", [128, NT], F32)
    sb("rstd", [128, NT], F32)
    sb("cq", [128, NT], F32)
    sb("m1536", [128, NT], F32)
    sb("sc_tmp", [128, NT], F32)
    sb("mx", [128, NT], F32)
    sb("mn", [128, NT], F32)
    sb("bnbuf", [128, 12], F32)
    sb("agbuf", [128, 2], F32)
    sb("amax1", [128, 1], F32)
    sb("g2mx", [128, 1], F32)
    sb("gath", [1, 128], F32)
    sb("ccpad", [1, 8], F32)
    sb("ccres", [1, 8], F32)
    sb("swg", [128, NI], F32)
    sb("swu", [128, NI], F32)
    sb("sdd", [128, ND], F32)
    sb("srow", [1, D], F32)
    sb("sd_b", [128, D], F32)
    sb("sdb_f", [128, D], F32)
    sb("cg", [128, NI], F32)
    sb("cu", [128, NI], F32)
    sb("sc_bank", [128, 8], F32)
    sb("scrow", [1, 8], F32)
    sb("wob_row", [1, D], F32)
    sb("tiny", [1, 8], F32)
    sb("onesall", [128, 64], F32)

    bd.set_base(nc._sbuf_addr_for_side("left"))
    assert SBUF_LIMIT - bd.base >= 160 * 1024, f"arena too small: {bd.base}"

    ps_f = es.enter_context(nc.psum_tensor("ps_f", [128, 6, 512], F32))
    ps_b = es.enter_context(nc.psum_tensor("ps_b", [128, 2, 1024], BF16))
    bd.dummy_ps = ps_f[:, 5, 0:64].bitcast(BF16)
    bd.dummy_id = st["c_idbf"][:]
    bd.dmy_v = st["bnbuf"][0:1, 0:1]
    bd.dmy_s = st["bnbuf"][0:1, 2:3]
    bd.dmy_g = st["bnbuf"][0:1, 4:5]

    if DELAY:
        dstart = gg.n + 1
        gg.n = dstart

        def gdelay(e, dstart=dstart):
            for _ in range(1000):
                e.nop(cycle_cnt=60000)
            e.nop().then_inc(gg.h, 1)
        bd.em("gpsimd", gdelay)
        for eng in ("sync", "vector", "scalar", "tensor"):
            bd.em(eng, lambda e, t=dstart: e.wait_ge(gg.h, t))

    def vhz():
        t = vv.n + 1
        vv.n = t
        bd.em("vector", lambda e, t=t: (
            e.memset(bd.dmy_v, 0.0).then_inc(vv.h, 1),
            e.wait_ge(vv.h, t)))

    def global_max(src_fn, idx):
        vready = vv.n
        t1_ = qg.tok()
        t2_ = qg.tok()

        def g1(e, vready=vready, t1_=t1_, t2_=t2_):
            e.wait_ge(vv.h, vready)
            e.dma_start(out=dm["gbounce"].ap(), in_=src_fn()
                        ).then_inc(t1_[0].h, 16)
            W(e, t1_)
            e.dma_start(out=st["gath"][:], in_=dm["gbounce"].ap()
                        ).then_inc(t2_[0].h, 16)
            W(e, t2_)
        bd.em("gpsimd", g1)

        vdone = vv.n + 2
        vv.n = vdone

        def v1(e, t2_=t2_, vdone=vdone):
            W(e, t2_)
            e.memset(st["ccpad"][:], 0.0).then_inc(vv.h, 1)
            e.wait_ge(vv.h, vdone - 1)
            e.reduce_max(st["ccpad"][0:1, 0:1], st["gath"][:],
                         axis=AX.X).then_inc(vv.h, 1)
        bd.em("vector", v1)

        t3_ = qg.tok()
        t4_ = qg.tok()
        gdone = gg.n + 1
        gg.n = gdone

        def g2(e, vdone=vdone, t3_=t3_, t4_=t4_, gdone=gdone, idx=idx):
            e.wait_ge(vv.h, vdone)
            e.dma_start(out=dm[f"ccin{idx}"].ap(), in_=st["ccpad"][:]
                        ).then_inc(t3_[0].h, 16)
            W(e, t3_)
            e.collective_compute(
                "AllReduce", ALU.max, replica_groups=[list(range(NCORE))],
                ins=[dm[f"ccin{idx}"].ap()], outs=[dm[f"ccout{idx}"].ap()],
            ).then_inc(gg.h, 1)
            e.wait_ge(gg.h, gdone)
            e.dma_start(out=st["ccres"][:], in_=dm[f"ccout{idx}"].ap()
                        ).then_inc(t4_[0].h, 16)
            W(e, t4_)
        bd.em("gpsimd", g2)
        return t4_

    def repl_scalars(row_fn, ncols, dst0, vwait):
        ta = qg.tok()
        tb = qg.tok()

        def g(e, vwait=vwait, ncols=ncols, dst0=dst0, ta=ta, tb=tb):
            e.wait_ge(vv.h, vwait)
            e.dma_start(out=dm["scbounce"].ap()[0:1, 0:ncols],
                        in_=row_fn()).then_inc(ta[0].h, 16)
            W(e, ta)
            e.dma_start(out=st["sc_bank"][:, dst0:dst0 + ncols],
                        in_=bass.AP(dm["scbounce"], 0, [[0, 128], [1, ncols]])
                        ).then_inc(tb[0].h, 16)
            W(e, tb)
        bd.em("gpsimd", g)

    def prologue():
        consts = [("c_idbf", "idbf"), ("sigm", "clip")]
        if STAGE >= 2:
            consts += [("c_t1", "t1"), ("c_t2", "t2"), ("c_idr", "idr"),
                       ("c_tri", "trimask"), ("wob_row", "wob")]
        ctoks = [qc.tok() for _ in consts]

        def sc0(e, ctoks=tuple(ctoks)):
            for (dn, sn), tk in zip(consts, ctoks):
                e.dma_start(out=st[dn][:], in_=dm[sn].ap()
                            ).then_inc(tk[0].h, 16)
            W(e, ctoks[-1])
            e.activation(st["sigm"][:], st["sigm"][:], AF.Sigmoid)
        bd.em("scalar", sc0)

        def v0(e):
            e.memset(st["c_eps"][:], EPS)
            e.memset(st["c_neg10"][:], EXPBIAS)
            e.memset(st["c_ones"][:], 1.0)
            e.memset(st["c_ones16"][:], 1.0)
            e.memset(st["onesall"][:], 1.0)
            e.tensor_copy(st["c_onesr"][:], st["c_ones16"][:])
        bd.em("vector", v0)
        bd.barrier()

        bd.phase_reset()
        wstage = bd.aa("p0_wstage", [128, 2, II], F32)
        w16 = bd.aa("p0_w16", [128, II], F16)
        wiq = bd.aa("p0_wiq", [128, 2, II], BF16)
        wtmp = bd.aa("p0_wtmp", [128, 2, II], BF16)
        amaxw = bd.aa("p0_amaxw", [128, 1], F32)
        r127 = bd.aa("p0_r127", [128, 1], F32)

        wsfree = [0, 0]
        wiqfree = [0, 0]
        wtmfree = [0, 0]
        psfree = [0, 0]

        def qwt(src_ap_fn, n_tiles, width, nchunk, scale_dst, dst_fn):
            for it in range(min(n_tiles, NWT)):
                buf = it % 2
                tq = qs.tok()

                def dmain(e, it=it, buf=buf, wf=wsfree[buf], width=width,
                          src_ap_fn=src_ap_fn, tq=tq):
                    e.wait_ge(vv.h, wf)
                    e.dma_start(out=wstage[:, buf, 0:width],
                                in_=src_ap_fn(it)).then_inc(tq[0].h, 16)
                bd.em("sync", dmain)

                if "Q" not in QSTEPS:
                    continue
                vq1 = vv.n + 3
                vv.n = vq1

                def vquant(e, it=it, buf=buf, tq=tq, width=width,
                           scale_dst=scale_dst, vq1=vq1):
                    W(e, tq)
                    w = wstage[:, buf, 0:width]
                    e.reduce_max(amaxw[:], w, axis=AX.X,
                                 apply_absolute_value=True).then_inc(vv.h, 1)
                    e.wait_ge(vv.h, vq1 - 2)
                    e.reciprocal(r127[:], amaxw[:])
                    e.tensor_scalar(st[scale_dst][:, it:it + 1], amaxw[:],
                                    1.0 / QMAX, None, ALU.mult
                                    ).then_inc(vv.h, 1)
                    e.wait_ge(vv.h, vq1 - 1)
                    e.tensor_scalar(r127[:], r127[:], QMAX, None, ALU.mult
                                    ).then_inc(vv.h, 1)
                bd.em("vector", vquant)
                vhz()

                vq2 = vv.n + 3
                vv.n = vq2

                def vquant2(e, buf=buf, width=width, wqf=wiqfree[buf],
                            vq2=vq2):
                    e.wait_ge(tv.h, wqf)
                    w = wstage[:, buf, 0:width]
                    e.tensor_scalar(w16[:, 0:width], w, r127[:], R16,
                                    ALU.mult, ALU.add).then_inc(vv.h, 1)
                    e.wait_ge(vv.h, vq2 - 2)
                    e.tensor_scalar(w16[:, 0:width], w16[:, 0:width],
                                    CLIP_LO, CLIP_HI, ALU.max,
                                    ALU.min).then_inc(vv.h, 1)
                    e.wait_ge(vv.h, vq2 - 1)
                    e.tensor_scalar(wiq[:, buf, 0:width], w16[:, 0:width],
                                    -R16, None, ALU.add).then_inc(vv.h, 1)
                bd.em("vector", vquant2)
                wsfree[buf] = vq2

                if "T" not in QSTEPS:
                    continue
                ngr = (nchunk + 7) // 8
                for g in range(ngr):
                    cnt = min(8, nchunk - g * 8)
                    ttgt = tv.n + 1
                    tv.n = ttgt

                    def tt(e, buf=buf, g=g, cnt=cnt, vq2=vq2, pf=psfree[g % 2]):
                        e.wait_ge(vv.h, max(vq2, pf))
                        inst = None
                        for c in range(cnt):
                            inst = e.transpose(
                                ps_b[:, g % 2, c * 128:(c + 1) * 128],
                                wiq[:, buf, (g * 8 + c) * 128:
                                    (g * 8 + c + 1) * 128],
                                st["c_idbf"][:])
                        inst.then_inc(tv.h, 1)
                    bd.em("tensor", tt)

                    if "E" not in QSTEPS:
                        continue
                    vev = vv.n + 1
                    vv.n = vev

                    def vevf(e, buf=buf, g=g, cnt=cnt, ttgt=ttgt,
                             wmf=wtmfree[buf]):
                        e.wait_ge(tv.h, ttgt)
                        if wmf != 0:
                            W(e, wmf)
                        e.tensor_copy(
                            wtmp[:, buf, g * 8 * 128:(g * 8 + cnt) * 128],
                            ps_b[:, g % 2, 0:cnt * 128]).then_inc(vv.h, 1)
                    bd.em("vector", vevf)
                    psfree[g % 2] = vev
                wiqfree[buf] = tv.n

                if "S" not in QSTEPS:
                    continue
                tqg = qg.tok()
                vneed = vv.n

                def gstore(e, it=it, buf=buf, vneed=vneed, dst_fn=dst_fn,
                           nchunk=nchunk, tqg=tqg):
                    e.wait_ge(vv.h, vneed)
                    dst = dst_fn(it)
                    innp = wtmp[:, buf, 0:nchunk * 128]
                    if len(dst.shape) == 3:
                        innp = innp.rearrange("p (c n) -> p c n", n=128)
                    e.dma_start(out=dst, in_=innp).then_inc(tqg[0].h, 16)
                bd.em("gpsimd", gstore)
                wtmfree[buf] = tqg

        wg_ap = dm["wg"].ap().rearrange("(t p) d -> t p d", p=128)
        wu_ap = dm["wu"].ap().rearrange("(t p) d -> t p d", p=128)
        wd_ap = dm["wd"].ap().rearrange("(t p) d -> t p d", p=128)
        if PUPTO < 1:
            bd.barrier()
            return
        qwt(lambda it: wg_ap[it], NI, D, ND, "swg",
            lambda it: dm["WGT"].ap()[it])
        if PUPTO >= 2:
            qwt(lambda it: wu_ap[it], NI, D, ND, "swu",
                lambda it: dm["WUT"].ap()[it])
        if PUPTO >= 3:
            qwt(lambda it: wd_ap[it], ND, II, NI, "sdd",
                lambda it: dm["IWD"].ap()[:, :, it * 128:(it + 1) * 128])
        bd.barrier()
        if PUPTO < 4:
            return

        t1_ = qg.tok()
        t2_ = qg.tok()

        def gsd(e, t1_=t1_, t2_=t2_):
            e.dma_start(out=dm["sdbounce"].ap(), in_=st["sdd"][:]
                        ).then_inc(t1_[0].h, 16)
            W(e, t1_)
            e.dma_start(out=st["srow"][:],
                        in_=bass.AP(dm["sdbounce"], 0,
                                    [[1, 1], [1, ND], [ND, 128]]),
                        allow_slow_non_contiguous=True
                        ).then_inc(t2_[0].h, 16)
            W(e, t2_)
        bd.em("gpsimd", gsd)

        t3_ = qg.tok()
        t4_ = qg.tok()

        def gsdb(e, t2_=t2_, t3_=t3_, t4_=t4_):
            W(e, t2_)
            e.dma_start(out=dm["rowbounce"].ap(), in_=st["srow"][:]
                        ).then_inc(t3_[0].h, 16)
            W(e, t3_)
            e.dma_start(out=st["sd_b"][:],
                        in_=bass.AP(dm["rowbounce"], 0, [[0, 128], [1, D]])
                        ).then_inc(t4_[0].h, 16)
            W(e, t4_)
        bd.em("gpsimd", gsdb)
        bd.barrier()

    def ffn(tag, src, dst, cc0):
        _u = [0]

        def done():
            _u[0] += 1
            return UPTO <= _u[0]

        bd.phase_reset()
        xst = bd.aa(f"{tag}_xst", [128, 2, 2, D], F32)
        sq2 = bd.aa(f"{tag}_sq2", [128, 2, D], F32)
        sqs = bd.aa(f"{tag}_sqs", [128, NT], F32)
        stfree = [0, 0]
        sqfree = [0, 0]
        for tp in range(NT // 2):
            buf = tp % 2
            tq = qs.tok()

            def dmx(e, tp=tp, buf=buf, sf=stfree[buf], src=src, tq=tq):
                e.wait_ge(vv.h, sf)
                e.dma_start(out=xst[:, buf],
                            in_=src.ap()[2 * tp:2 * tp + 2].transpose([1, 0, 2])
                            ).then_inc(tq[0].h, 16)
            bd.em("sync", dmx)

            for k in range(2):
                t = 2 * tp + k
                sdone = ss.n + 1
                ss.n = sdone

                def sstat(e, t=t, buf=buf, k=k, tq=tq, sqf=sqfree[t % 2]):
                    W(e, tq)
                    e.wait_ge(vv.h, sqf)
                    e.activation(sq2[:, t % 2], xst[:, buf, k],
                                 AF.Square).then_inc(ss.h, 1)
                bd.em("scalar", sstat)

                vdone = vv.n + 1
                vv.n = vdone
                sqfree[t % 2] = vdone

                def vstat(e, t=t, buf=buf, k=k, tq=tq, sdone=sdone):
                    W(e, tq)
                    tile = xst[:, buf, k]
                    e.reduce_sum(st["negm"][:, t:t + 1], tile, axis=AX.X)
                    e.wait_ge(ss.h, sdone)
                    e.reduce_sum(sqs[:, t:t + 1], sq2[:, t % 2], axis=AX.X)
                    e.reduce_max(st["mx"][:, t:t + 1], tile, axis=AX.X)
                    e.tensor_reduce(st["mn"][:, t:t + 1], tile, axis=AX.X,
                                    op=ALU.min).then_inc(vv.h, 1)
                bd.em("vector", vstat)
            stfree[buf] = vv.n

        base = vv.n
        vv.n = base + 6

        def vvar(e, base=base):
            e.tensor_scalar(st["negm"][:], st["negm"][:], -1.0 / D, None,
                            ALU.mult).then_inc(vv.h, 1)
            e.wait_ge(vv.h, base + 1)
            e.tensor_tensor(st["var"][:], st["negm"][:], st["negm"][:],
                            ALU.mult)
            e.tensor_scalar(st["sc_tmp"][:], sqs[:], 1.0 / D, None,
                            ALU.mult).then_inc(vv.h, 1)
            e.wait_ge(vv.h, base + 2)
            e.tensor_tensor(st["var"][:], st["sc_tmp"][:], st["var"][:],
                            ALU.subtract)
            e.tensor_tensor(st["mx"][:], st["mx"][:], st["negm"][:], ALU.add)
            e.tensor_tensor(st["mn"][:], st["mn"][:], st["negm"][:],
                            ALU.add).then_inc(vv.h, 1)
            e.wait_ge(vv.h, base + 3)
            e.tensor_scalar(st["mn"][:], st["mn"][:], -1.0, None,
                            ALU.mult).then_inc(vv.h, 1)
            e.wait_ge(vv.h, base + 4)
            e.tensor_tensor(st["mx"][:], st["mx"][:], st["mn"][:],
                            ALU.max).then_inc(vv.h, 2)
        bd.em("vector", vvar)
        vneed = vv.n

        ssn = ss.n + 1
        ss.n = ssn

        def srstd(e, vneed=vneed):
            e.wait_ge(vv.h, vneed)
            e.activation(st["rstd"][:], st["var"][:], AF.Sqrt,
                         bias=st["c_eps"][:]).then_inc(ss.h, 1)
        bd.em("scalar", srstd)

        base = vv.n
        vv.n = base + 3

        def vamax(e, ssn=ssn, base=base):
            e.wait_ge(ss.h, ssn)
            e.reciprocal(st["rstd"][:], st["rstd"][:]).then_inc(vv.h, 1)
            e.wait_ge(vv.h, base + 1)
            e.tensor_tensor(st["mx"][:], st["mx"][:], st["rstd"][:],
                            ALU.mult).then_inc(vv.h, 1)
            e.wait_ge(vv.h, base + 2)
            e.reduce_max(st["amax1"][:], st["mx"][:],
                         axis=AX.X).then_inc(vv.h, 1)
        bd.em("vector", vamax)

        qtgt = global_max(lambda: st["amax1"][:], cc0)

        vdone = vv.n + 3
        vv.n = vdone

        def vscale(e, qtgt=qtgt, vdone=vdone):
            W(e, qtgt)
            r = st["scrow"]
            e.tensor_tensor(r[0:1, 1:2], st["sigm"][0:1, 0:1],
                            st["ccres"][0:1, 0:1], ALU.mult).then_inc(vv.h, 1)
            e.wait_ge(vv.h, vdone - 2)
            e.tensor_scalar(r[0:1, 1:2], r[0:1, 1:2], 1.0 / QMAX, None,
                            ALU.mult).then_inc(vv.h, 1)
            e.wait_ge(vv.h, vdone - 1)
            e.reciprocal(r[0:1, 0:1], r[0:1, 1:2]).then_inc(vv.h, 1)
        bd.em("vector", vscale)
        repl_scalars(lambda: st["scrow"][0:1, 0:2], 2, 0, vdone)
        bd.barrier()
        if done():
            if DBG:
                dt1 = qg.tok()
                dt2 = qg.tok()
                dt3 = qg.tok()

                def vdbg(e):
                    e.memset(st["sc_tmp"][:], 7.0).then_inc(vv.h, 1)
                bd.em("vector", vdbg)
                vv.n += 1
                _vd = vv.n

                def gdbg(e, dt1=dt1, dt2=dt2, dt3=dt3, dst=dst, _vd=_vd):
                    e.wait_ge(vv.h, _vd)
                    e.dma_start(out=dst.ap()[6, :, 0:NT],
                                in_=st["sc_tmp"][:]).then_inc(dt1[0].h, 16)
                    e.dma_start(out=dst.ap()[0, :, 0:8],
                                in_=st["sc_bank"][:]).then_inc(dt1[0].h, 16)
                    e.dma_start(out=dst.ap()[0, 0:1, 8:16],
                                in_=st["ccres"][:]).then_inc(dt2[0].h, 16)
                    e.dma_start(out=dst.ap()[0, 0:1, 16:144],
                                in_=st["gath"][:]).then_inc(dt3[0].h, 16)
                    dt4 = None
                    e.dma_start(out=dst.ap()[1, :, 0:NT],
                                in_=st["negm"][:]).then_inc(dt1[0].h, 16)
                    e.dma_start(out=dst.ap()[2, :, 0:NT],
                                in_=st["mx"][:]).then_inc(dt2[0].h, 16)
                    e.dma_start(out=dst.ap()[3, :, 0:D],
                                in_=xst[:, 0, 0]).then_inc(dt3[0].h, 16)
                    e.dma_start(out=dst.ap()[4, 0:1, 0:2],
                                in_=st["sigm"][:]).then_inc(dt1[0].h, 16)
                    e.dma_start(out=xst[:, 0, 0],
                                in_=src.ap()[5]).then_inc(dt2[0].h, 16)
                    e.wait_ge(dt2[0].h, dt2[1] + 32)
                    e.dma_start(out=dst.ap()[5, :, 0:D],
                                in_=xst[:, 0, 0]).then_inc(dt3[0].h, 16)
                    e.wait_ge(dt1[0].h, dt1[1] + 32)
                    e.wait_ge(dt2[0].h, dt2[1] + 32)
                    e.wait_ge(dt3[0].h, dt3[1] + 48)
                bd.em("gpsimd", gdbg)
            return

        base = vv.n
        vv.n = base + 2

        def vcq(e, base=base):
            e.tensor_scalar(st["cq"][:], st["rstd"][:], st["sc_bank"][:, 0:1],
                            None, ALU.mult)
            e.tensor_scalar(st["cg"][:], st["swg"][:], st["sc_bank"][:, 1:2],
                            None, ALU.mult)
            e.tensor_scalar(st["cu"][:], st["swu"][:], st["sc_bank"][:, 1:2],
                            None, ALU.mult).then_inc(vv.h, 1)
            e.wait_ge(vv.h, base + 1)
            e.tensor_tensor(st["m1536"][:], st["negm"][:], st["cq"][:],
                            ALU.mult).then_inc(vv.h, 1)
            e.wait_ge(vv.h, base + 2)
            e.tensor_scalar(st["m1536"][:], st["m1536"][:], R16, None, ALU.add)
        bd.em("vector", vcq)
        bd.barrier()

        bd.phase_reset()
        xq_fm = bd.aa(f"{tag}_xqfm", [128, ND, T], BF16)
        keep = bd.arena - bd.base
        xst2 = bd.aa(f"{tag}_xst2", [128, 2, 2, D], F32)
        xq16 = bd.aa(f"{tag}_xq16", [128, D], F16)
        xqt = bd.aa(f"{tag}_xqt", [128, 2, D], BF16)
        st2free = [0, 0]
        xqtfree = [0, 0]
        psxf = [0, 0]
        for tp in range(NT // 2):
            buf = tp % 2
            tq = qs.tok()

            def dmx2(e, tp=tp, buf=buf, sf=st2free[buf], src=src, tq=tq):
                e.wait_ge(vv.h, sf)
                e.dma_start(out=xst2[:, buf],
                            in_=src.ap()[2 * tp:2 * tp + 2].transpose([1, 0, 2])
                            ).then_inc(tq[0].h, 16)
            bd.em("sync", dmx2)

            for k in range(2):
                t = 2 * tp + k
                vdone = vv.n + 3
                vv.n = vdone

                def vq(e, t=t, buf=buf, k=k, tq=tq, xf=xqtfree[k], vd=vdone):
                    W(e, tq)
                    e.wait_ge(tv.h, xf)
                    tile = xst2[:, buf, k]
                    e.tensor_scalar(xq16[:], tile, st["cq"][:, t:t + 1],
                                    st["m1536"][:, t:t + 1], ALU.mult, ALU.add
                                    ).then_inc(vv.h, 1)
                    e.wait_ge(vv.h, vd - 2)
                    e.tensor_scalar(xq16[:], xq16[:], CLIP_LO, CLIP_HI,
                                    ALU.max, ALU.min).then_inc(vv.h, 1)
                    e.wait_ge(vv.h, vd - 1)
                    e.tensor_scalar(xqt[:, k], xq16[:], -R16, None,
                                    ALU.add).then_inc(vv.h, 1)
                bd.em("vector", vq)

                ttgt = tv.n + 1
                tv.n = ttgt

                def txq(e, t=t, k=k, vdone=vdone, pf=psxf[t % 2]):
                    e.wait_ge(vv.h, max(vdone, pf))
                    inst = None
                    for c in range(ND):
                        inst = e.transpose(
                            ps_b[:, t % 2, c * 128:(c + 1) * 128],
                            xqt[:, k, c * 128:(c + 1) * 128],
                            st["c_idbf"][:])
                    inst.then_inc(tv.h, 1)
                bd.em("tensor", txq)
                xqtfree[k] = ttgt

                vev = vv.n + 1
                vv.n = vev

                def vevx(e, t=t, ttgt=ttgt):
                    e.wait_ge(tv.h, ttgt)
                    e.tensor_copy(
                        xq_fm.ap()[:, :, t * 128:(t + 1) * 128],
                        ps_b[:, t % 2, 0:1024].rearrange("p (c n) -> p c n",
                                                         n=128)).then_inc(vv.h, 1)
                bd.em("vector", vevx)
                psxf[t % 2] = vev
            st2free[buf] = vv.n
        bd.barrier()
        if done():
            return

        bd.phase_reset(keep)
        wgt = bd.aa(f"{tag}_wgt", [128, 2, 2 * ND, 128], BF16)
        silu = bd.aa(f"{tag}_silu", [128, 2, 512], F32)
        g2ev = bd.aa(f"{tag}_g2ev", [128, 2, T], F32)
        wfree = [0, 0]
        g2dfree = [0, 0]
        gupsfree = [0, 0]

        def vg2mx0(e):
            e.memset(st["g2mx"][:], 0.0)
        bd.em("vector", vg2mx0)

        for ic in range(NI):
            wbuf = ic % 2
            tqa = qs.tok()
            tqb = qs.tok()

            def dmw(e, ic=ic, wbuf=wbuf, tf=wfree[wbuf], tqa=tqa, tqb=tqb):
                e.wait_ge(tv.h, tf)
                e.dma_start(out=wgt[:, wbuf, 0:ND],
                            in_=dm["WGT"].ap()[ic]
                            .rearrange("p (c n) -> p c n", n=128)
                            ).then_inc(tqa[0].h, 16)
                e.dma_start(out=wgt[:, wbuf, ND:2 * ND],
                            in_=dm["WUT"].ap()[ic]
                            .rearrange("p (c n) -> p c n", n=128)
                            ).then_inc(tqb[0].h, 16)
            bd.em("sync", dmw)

            for tb in range(8):
                par = tb % 2
                ttgt = tv.n + 1
                tv.n = ttgt

                def tgu(e, ic=ic, tb=tb, par=par, wbuf=wbuf, tqa=tqa,
                        tqb=tqb, pf=gupsfree[par]):
                    W(e, tqa)
                    W(e, tqb)
                    if pf:
                        e.wait_ge(vv.h, pf)
                    for hc in range(ND):
                        e.matmul(ps_f[:, par, :], wgt[:, wbuf, hc],
                                 xq_fm[:, hc, tb * 512:(tb + 1) * 512],
                                 start=(hc == 0), stop=(hc == ND - 1))
                    inst = None
                    for hc in range(ND):
                        inst = e.matmul(
                            ps_f[:, 2 + par, :], wgt[:, wbuf, ND + hc],
                            xq_fm[:, hc, tb * 512:(tb + 1) * 512],
                            start=(hc == 0), stop=(hc == ND - 1))
                    inst.then_inc(tv.h, 1)
                bd.em("tensor", tgu)

                sdone = sv.n + 1
                sv.n = sdone

                def ssilu(e, ic=ic, par=par, ttgt=ttgt):
                    e.wait_ge(tv.h, ttgt)
                    e.activation(silu[:, par], ps_f[:, par, :], AF.Silu,
                                 scale=st["cg"][:, ic:ic + 1]).then_inc(sv.h, 1)
                bd.em("scalar", ssilu)

                vdone = vv.n + 1
                vv.n = vdone

                def vg2(e, ic=ic, tb=tb, par=par, sdone=sdone,
                        gf=g2dfree[ic % 2]):
                    e.wait_ge(sv.h, sdone)
                    if gf != 0:
                        W(e, gf)
                    e.scalar_tensor_tensor(
                        g2ev[:, ic % 2, tb * 512:(tb + 1) * 512],
                        ps_f[:, 2 + par, :], st["cu"][:, ic:ic + 1],
                        silu[:, par], ALU.mult, ALU.mult).then_inc(vv.h, 1)
                bd.em("vector", vg2)
                gupsfree[par] = vdone

            vdone = vv.n + 2
            vv.n = vdone

            def vmax(e, ic=ic, vdone=vdone):
                e.tensor_reduce(st["amax1"][:], g2ev[:, ic % 2, :], axis=AX.X,
                                op=ALU.max,
                                apply_absolute_value=True).then_inc(vv.h, 1)
                e.wait_ge(vv.h, vdone - 1)
                e.tensor_tensor(st["g2mx"][:], st["g2mx"][:], st["amax1"][:],
                                ALU.max).then_inc(vv.h, 1)
            bd.em("vector", vmax)

            tqg = qg.tok()
            vneed = vv.n

            def gg2(e, ic=ic, vneed=vneed, tqg=tqg):
                e.wait_ge(vv.h, vneed)
                e.dma_start(out=dm["G2"].ap()[ic], in_=g2ev[:, ic % 2, :]
                            ).then_inc(tqg[0].h, 16)
            bd.em("gpsimd", gg2)
            g2dfree[ic % 2] = tqg
            wfree[wbuf] = tv.n
        bd.barrier()
        if done():
            return

        qtgt2 = global_max(lambda: st["g2mx"][:], cc0 + 1)
        vdone = vv.n + 4
        vv.n = vdone

        def vscale2(e, qtgt2=qtgt2, vdone=vdone):
            W(e, qtgt2)
            r = st["scrow"]
            e.tensor_tensor(r[0:1, 3:4], st["sigm"][0:1, 1:2],
                            st["ccres"][0:1, 0:1], ALU.mult).then_inc(vv.h, 1)
            e.wait_ge(vv.h, vdone - 3)
            e.tensor_scalar(r[0:1, 3:4], r[0:1, 3:4], 1.0 / QMAX, None,
                            ALU.mult).then_inc(vv.h, 1)
            e.wait_ge(vv.h, vdone - 2)
            e.reciprocal(r[0:1, 2:3], r[0:1, 3:4]).then_inc(vv.h, 1)
            e.wait_ge(vv.h, vdone - 1)
            e.tensor_scalar(r[0:1, 3:4], r[0:1, 3:4], 0.5, None,
                            ALU.mult).then_inc(vv.h, 1)
        bd.em("vector", vscale2)
        repl_scalars(lambda: st["scrow"][0:1, 2:4], 2, 2, vdone)
        bd.barrier()

        def vsdbf(e):
            e.tensor_scalar(st["sdb_f"][:], st["sd_b"][:],
                            st["sc_bank"][:, 3:4], None, ALU.mult)
        bd.em("vector", vsdbf)
        bd.barrier()
        if done():
            return

        bd.phase_reset()
        iwd = bd.aa(f"{tag}_iwd", [128, NI, D], BF16)
        g2st = bd.aa(f"{tag}_g2st", [128, 2, 11, 512], F32)
        h1q = bd.aa(f"{tag}_h1q", [128, NI, 512], BF16)
        h16 = bd.aa(f"{tag}_h16", [128, 11, 512], F16)
        xin = bd.aa(f"{tag}_xin", [128, 4, D], F32)
        xout = bd.aa(f"{tag}_xout", [128, 4, D], F32)
        tmp5 = bd.aa(f"{tag}_tmp5", [128, 512], F32)

        tiwd = qs.tok()

        def dmiwd(e, tiwd=tiwd):
            e.dma_start(out=iwd[:], in_=dm["IWD"].ap()).then_inc(tiwd[0].h, 16)
        bd.em("sync", dmiwd)

        h1qfree = 0
        dpsfree = [0, 0]
        xodfree = 0
        xinfree = 0
        g2stfree = 0

        for ts_ in range(8):
            sb_ = ts_ % 2
            tg0 = qs.tok()
            tg1 = qs.tok()
            txq_ = qs.tok()

            def dmg2(e, ts_=ts_, gf=g2stfree, xf=xinfree, src=src,
                     toks=(tg0, tg1, txq_)):
                e.wait_ge(vv.h, max(gf, xf))
                for half in range(2):
                    e.dma_start(
                        out=g2st[:, half],
                        in_=dm["G2"].ap()[half * 11:(half + 1) * 11, :,
                                          ts_ * 512:(ts_ + 1) * 512]
                        .transpose([1, 0, 2])).then_inc(toks[half][0].h, 16)
                e.dma_start(out=xin[:],
                            in_=src.ap()[ts_ * 4:(ts_ + 1) * 4]
                            .transpose([1, 0, 2])).then_inc(toks[2][0].h, 16)
            bd.em("sync", dmg2)

            vq5 = vv.n + 6
            vv.n = vq5

            def vh(e, ts_=ts_, hqf=h1qfree, toks=(tg0, tg1), vq5=vq5):
                e.wait_ge(tv.h, hqf)
                for half in range(2):
                    b0 = vq5 - 6 + 3 * half
                    W(e, toks[half])
                    g = g2st[:, half].rearrange("p a b -> p (a b)")
                    hh = h16[:].rearrange("p a b -> p (a b)")
                    e.tensor_scalar(hh, g, st["sc_bank"][:, 2:3], R16,
                                    ALU.mult, ALU.add).then_inc(vv.h, 1)
                    e.wait_ge(vv.h, b0 + 1)
                    e.tensor_scalar(hh, hh, CLIP_LO, CLIP_HI, ALU.max,
                                    ALU.min).then_inc(vv.h, 1)
                    e.wait_ge(vv.h, b0 + 2)
                    e.tensor_scalar(
                        h1q[:, half * 11:(half + 1) * 11]
                        .rearrange("p a b -> p (a b)"),
                        hh, -R16, None, ALU.add).then_inc(vv.h, 1)
            bd.em("vector", vh)
            g2stfree = vq5

            for tb4 in range(4):
                par = tb4 % 2
                ttgt = tv.n + 1
                tv.n = ttgt

                def tdown(e, ts_=ts_, tb4=tb4, par=par, vq5=vq5,
                          pf=dpsfree[par], tiwd=tiwd):
                    W(e, tiwd)
                    e.wait_ge(vv.h, max(vq5, pf))
                    inst = None
                    for ic in range(NI):
                        lhs = h1q[:, ic, tb4 * 128:(tb4 + 1) * 128]
                        e.matmul(ps_f[:, 2 * par, :], lhs, iwd[:, ic, 0:512],
                                 start=(ic == 0), stop=(ic == NI - 1))
                        inst = e.matmul(ps_f[:, 2 * par + 1, :], lhs,
                                        iwd[:, ic, 512:1024],
                                        start=(ic == 0), stop=(ic == NI - 1))
                    inst.then_inc(tv.h, 1)
                bd.em("tensor", tdown)

                vdone = vv.n + 3
                vv.n = vdone

                def vdwn(e, tb4=tb4, par=par, ttgt=ttgt, txq_=txq_,
                         gf=xodfree, vd=vdone):
                    e.wait_ge(tv.h, ttgt)
                    W(e, txq_)
                    if gf != 0:
                        W(e, gf)
                    e.tensor_tensor(tmp5[:], ps_f[:, 2 * par, :],
                                    st["sdb_f"][:, 0:512],
                                    ALU.mult).then_inc(vv.h, 1)
                    e.wait_ge(vv.h, vd - 2)
                    e.tensor_tensor(xout[:, tb4, 0:512], tmp5[:],
                                    xin[:, tb4, 0:512], ALU.add)
                    e.tensor_tensor(tmp5[:], ps_f[:, 2 * par + 1, :],
                                    st["sdb_f"][:, 512:1024],
                                    ALU.mult).then_inc(vv.h, 1)
                    e.wait_ge(vv.h, vd - 1)
                    e.tensor_tensor(xout[:, tb4, 512:1024], tmp5[:],
                                    xin[:, tb4, 512:1024],
                                    ALU.add).then_inc(vv.h, 1)
                bd.em("vector", vdwn)
                dpsfree[par] = vdone

            h1qfree = tv.n
            xinfree = vv.n
            tqg = qg.tok()
            vneed = vv.n

            def gxout(e, ts_=ts_, vneed=vneed, dst=dst, tqg=tqg):
                e.wait_ge(vv.h, vneed)
                e.dma_start(out=dst.ap()[ts_ * 4:(ts_ + 1) * 4]
                            .transpose([1, 0, 2]),
                            in_=xout[:]).then_inc(tqg[0].h, 16)
            bd.em("gpsimd", gxout)
            xodfree = tqg
        bd.barrier()


    def attn(tag, src, dst):
        bd.phase_reset()
        z_fm = bd.aa(f"{tag}_zfm", [128, ND, S], F32R)
        keepz = bd.arena - bd.base
        za = bd.aa(f"{tag}_za", [128, 2, D], F32)
        sqa = bd.aa(f"{tag}_sqa", [128, 2, D], F32)
        zt = bd.aa(f"{tag}_zt", [128, 2, D], F32R)
        s1a = bd.aa(f"{tag}_s1a", [128, 8], F32)
        qraw = bd.aa(f"{tag}_qraw", [128, S], F32)
        qsw = bd.aa(f"{tag}_qsw", [128, S], F32)
        qr = bd.aa(f"{tag}_qr", [128, S], F32R)
        kr = bd.aa(f"{tag}_kr", [128, S], F32R)
        wqp = bd.aa(f"{tag}_wqp", [128, 2, 3, ND, 128], F32R)
        vfm = bd.aa(f"{tag}_vfm", [128, S], F32R)
        vtok = bd.aa(f"{tag}_vtok", [128, TPB * 130], F32R)
        pt = bd.aa(f"{tag}_pt", [128, 2, 512], F32R)
        recipb = bd.aa(f"{tag}_recipb", [128, 512], F32)
        aoh = bd.alias(f"{tag}_aoh", [128, S], F32R, f"{tag}_qsw")
        projfree = [0, 0]
        projfree2 = [0, 0]
        accfree = [None]
        oxoutfree = [0, 0]
        pvfree = [0]

        def emit_wload(p):
            toks = [qs.tok() for _ in range(3)]

            def dmwp(e, p=p, toks=tuple(toks)):
                for i, wnm in enumerate(("wq", "wk", "wv")):
                    e.dma_start(out=wqp[:, p % 2, i],
                                in_=dm[wnm].ap()[p].transpose([1, 0, 2])
                                ).then_inc(toks[i][0].h, 16)
            bd.em("sync", dmwp)
            return toks

        for b in range(BC):
            tb0 = b * TPB
            wtoks = {0: emit_wload(0)}
            zafree = [0, 0]
            zpsfree = [0, 0]
            for t_ in range(TPB):
                buf = t_ % 2
                tqz = qs.tok()

                def dmz(e, t_=t_, buf=buf, zf=zafree[buf], tqz=tqz, tb0=tb0,
                        src=src):
                    e.wait_ge(vv.h, zf)
                    e.dma_start(out=za[:, buf], in_=src.ap()[tb0 + t_]
                                ).then_inc(tqz[0].h, 16)
                bd.em("sync", dmz)

                sq_d = ss.n + 1
                ss.n = sq_d

                def sstz(e, buf=buf, tqz=tqz):
                    W(e, tqz)
                    e.activation(sqa[:, buf], za[:, buf],
                                 AF.Square).then_inc(ss.h, 1)
                bd.em("scalar", sstz)

                vz = vv.n + 5
                vv.n = vz
                s2_d = ss.n + 1
                ss.n = s2_d

                def vstz(e, buf=buf, tqz=tqz, vz=vz, sq_d=sq_d):
                    W(e, tqz)
                    e.reduce_sum(s1a[:, 0:1], za[:, buf],
                                 axis=AX.X).then_inc(vv.h, 1)
                    e.wait_ge(ss.h, sq_d)
                    e.reduce_sum(s1a[:, 1:2], sqa[:, buf],
                                 axis=AX.X).then_inc(vv.h, 1)
                    e.wait_ge(vv.h, vz - 3)
                    e.tensor_scalar(s1a[:, 2:3], s1a[:, 0:1], -1.0 / D, None,
                                    ALU.mult)
                    e.tensor_scalar(s1a[:, 3:4], s1a[:, 1:2], 1.0 / D, None,
                                    ALU.mult).then_inc(vv.h, 1)
                    e.wait_ge(vv.h, vz - 2)
                    e.tensor_tensor(s1a[:, 4:5], s1a[:, 2:3], s1a[:, 2:3],
                                    ALU.mult).then_inc(vv.h, 1)
                    e.wait_ge(vv.h, vz - 1)
                    e.tensor_tensor(s1a[:, 3:4], s1a[:, 3:4], s1a[:, 4:5],
                                    ALU.subtract).then_inc(vv.h, 1)
                bd.em("vector", vstz)

                def srs(e, vz=vz):
                    e.wait_ge(vv.h, vz)
                    e.activation(s1a[:, 5:6], s1a[:, 3:4], AF.Sqrt,
                                 bias=st["c_eps"][:]).then_inc(ss.h, 1)
                bd.em("scalar", srs)

                vz2 = vv.n + 3
                vv.n = vz2

                def vnz(e, buf=buf, vz2=vz2, s2_d=s2_d):
                    e.wait_ge(ss.h, s2_d)
                    e.reciprocal(s1a[:, 6:7], s1a[:, 5:6]).then_inc(vv.h, 1)
                    e.wait_ge(vv.h, vz2 - 2)
                    e.tensor_tensor(s1a[:, 7:8], s1a[:, 2:3], s1a[:, 6:7],
                                    ALU.mult).then_inc(vv.h, 1)
                    e.wait_ge(vv.h, vz2 - 1)
                    e.tensor_scalar(zt[:, buf], za[:, buf], s1a[:, 6:7],
                                    s1a[:, 7:8], ALU.mult,
                                    ALU.add).then_inc(vv.h, 1)
                bd.em("vector", vnz)

                ttz = tv.n + 1
                tv.n = ttz

                def tz(e, t_=t_, buf=buf, vz2=vz2, pf=zpsfree[t_ % 2]):
                    e.wait_ge(vv.h, max(vz2, pf))
                    psz = ps_f[:, 2 * (t_ % 2):2 * (t_ % 2) + 2, :]\
                        .rearrange("p a b -> p (a b)").bitcast(F32R)
                    inst = None
                    for c in range(ND):
                        inst = e.transpose(
                            psz[:, c * 128:(c + 1) * 128],
                            zt[:, buf, c * 128:(c + 1) * 128], st["c_idr"][:])
                    inst.then_inc(tv.h, 1)
                bd.em("tensor", tz)

                vz3 = vv.n + 1
                vv.n = vz3

                def vez(e, t_=t_, ttz=ttz):
                    e.wait_ge(tv.h, ttz)
                    psz = ps_f[:, 2 * (t_ % 2):2 * (t_ % 2) + 2, :]\
                        .rearrange("p a b -> p (a b)").bitcast(F32R)
                    e.tensor_copy(
                        z_fm.ap()[:, :, t_ * 128:(t_ + 1) * 128],
                        psz.rearrange("p (c n) -> p c n", n=128)
                        ).then_inc(vv.h, 1)
                bd.em("vector", vez)
                zpsfree[t_ % 2] = vz3
                zafree[buf] = vz3
            bd.barrier()

            for p in range(NPAIR):
                if p > 0:
                    bd.barrier()
                if p + 1 < NPAIR:
                    wtoks[p + 1] = emit_wload(p + 1)
                twp3 = wtoks[p]

                for which, outb in ((0, qr), (1, kr)):
                    for qt4 in range(4):
                        par = qt4 % 2
                        tpq = tv.n + 1
                        tv.n = tpq

                        def tproj(e, which=which, qt4=qt4, par=par, p=p,
                                  twp=twp3[which], pf=projfree[par]):
                            W(e, twp)
                            if pf:
                                e.wait_ge(vv.h, pf)
                            inst = None
                            for dc in range(ND):
                                inst = e.matmul(
                                    ps_f[:, par, :], wqp[:, p % 2, which, dc],
                                    z_fm[:, dc, qt4 * 512:(qt4 + 1) * 512],
                                    start=(dc == 0), stop=(dc == ND - 1))
                            inst.then_inc(tv.h, 1)
                        bd.em("tensor", tproj)

                        vpq = vv.n + 1
                        vv.n = vpq

                        def vproj(e, qt4=qt4, par=par, tpq=tpq):
                            e.wait_ge(tv.h, tpq)
                            e.tensor_copy(qraw[:, qt4 * 512:(qt4 + 1) * 512],
                                          ps_f[:, par, :]).then_inc(vv.h, 1)
                        bd.em("vector", vproj)
                        projfree[par] = vpq

                    tsw4 = [qs.tok() for _ in range(4)]
                    vneed_q = vv.n

                    def dmsw(e, toks=tuple(tsw4), vneed_q=vneed_q):
                        e.wait_ge(vv.h, vneed_q)
                        for blk in range(4):
                            sp = [32, 0, 96, 64][blk]
                            e.dma_start(
                                out=qsw[blk * 32:(blk + 1) * 32, :],
                                in_=qraw[sp:sp + 32, :]
                                ).then_inc(toks[blk][0].h, 16)
                    bd.em("sync", dmsw)

                    vr = vv.n + 2
                    vv.n = vr

                    def vrope(e, outb=outb, toks=tuple(tsw4), vr=vr):
                        for tok in toks:
                            W(e, tok)
                        e.tensor_tensor(qraw[:], qraw[:], st["c_t1"][:],
                                        ALU.mult)
                        e.tensor_tensor(qsw[:], qsw[:], st["c_t2"][:],
                                        ALU.mult).then_inc(vv.h, 1)
                        e.wait_ge(vv.h, vr - 1)
                        e.tensor_tensor(outb[:], qraw[:], qsw[:],
                                        ALU.add).then_inc(vv.h, 1)
                    bd.em("vector", vrope)
                ropedone = vv.n

                for vt4 in range(4):
                    par = vt4 % 2
                    tpv = tv.n + 1
                    tv.n = tpv

                    def tprojv(e, vt4=vt4, par=par, p=p, twp=twp3[2],
                               pf=projfree[par]):
                        W(e, twp)
                        if pf:
                            e.wait_ge(vv.h, pf)
                        inst = None
                        for dc in range(ND):
                            inst = e.matmul(
                                ps_f[:, par, :], wqp[:, p % 2, 2, dc],
                                z_fm[:, dc, vt4 * 512:(vt4 + 1) * 512],
                                start=(dc == 0), stop=(dc == ND - 1))
                        inst.then_inc(tv.h, 1)
                    bd.em("tensor", tprojv)

                    vpv = vv.n + 1
                    vv.n = vpv

                    def vprojv(e, vt4=vt4, par=par, tpv=tpv):
                        e.wait_ge(tv.h, tpv)
                        e.tensor_copy(vfm[:, vt4 * 512:(vt4 + 1) * 512],
                                      ps_f[:, par, :]).then_inc(vv.h, 1)
                    bd.em("vector", vprojv)
                    projfree[par] = vpv
                vfmdone = vv.n

                vo = vv.n + 1
                vv.n = vo

                def vones(e, vo=vo):
                    src3 = bass.AP(st["c_onesr"], 0,
                                   [[16, 128], [1, TPB], [1, 1]])
                    e.tensor_copy(
                        bass.AP(vtok, 64,
                                [[TPB * 130, 128], [130, TPB], [1, 1]]),
                        src3)
                    e.tensor_copy(
                        bass.AP(vtok, 129,
                                [[TPB * 130, 128], [130, TPB], [1, 1]]),
                        src3).then_inc(vv.h, 1)
                bd.em("vector", vones)

                for kc2 in range(TPB):
                    par = kc2 % 2
                    tvt_ = tv.n + 1
                    tv.n = tvt_

                    def tvtr(e, kc2=kc2, par=par, vfmdone=vfmdone,
                             pf=projfree2[kc2 % 2]):
                        e.wait_ge(vv.h, max(vfmdone, pf))
                        e.transpose(ps_f[:, 2 + par, 0:128].bitcast(F32R),
                                    vfm[:, kc2 * 128:(kc2 + 1) * 128],
                                    st["c_idr"][:]).then_inc(tv.h, 1)
                    bd.em("tensor", tvtr)

                    vvt = vv.n + 1
                    vv.n = vvt

                    def vevt(e, kc2=kc2, par=par, tvt_=tvt_):
                        e.wait_ge(tv.h, tvt_)
                        e.tensor_copy(
                            bass.AP(vtok, kc2 * 130,
                                    [[TPB * 130, 128], [65, 2], [1, 64]]),
                            ps_f[:, 2 + par, 0:128].bitcast(F32R)
                            .rearrange("p (a n) -> p a n", n=64)
                            ).then_inc(vv.h, 1)
                    bd.em("vector", vevt)
                    projfree2[kc2 % 2] = vvt
                vtokdone = vv.n

                for hh in range(2):
                    hbase = hh * 64
                    head = p * 2 + hh
                    for w in range(4):
                        nkc = 4 * w + 4
                        exp_t = {}
                        pv_t = {}
                        last_pv = None
                        pvfree_w = pvfree[0]
                        for kc in range(nkc):
                            diag = kc >= 4 * w
                            tS = tv.n + 1
                            tv.n = tS

                            def tqk(e, kc=kc, w=w, hbase=hbase, diag=diag,
                                    ef=exp_t.get(kc - 2, 0),
                                    rd=max(ropedone, vtokdone)):
                                e.wait_ge(vv.h, rd)
                                if ef:
                                    e.wait_ge(sv.h, ef)
                                inst = e.matmul(
                                    ps_f[:, kc % 2, :],
                                    kr[hbase:hbase + 64,
                                       kc * 128:(kc + 1) * 128],
                                    qr[hbase:hbase + 64,
                                       w * 512:(w + 1) * 512],
                                    start=True, stop=not diag)
                                if diag:
                                    off = 384 - 128 * (kc - 4 * w)
                                    inst = e.matmul(
                                        ps_f[:, kc % 2, :], st["c_idr"][:],
                                        st["c_tri"][:, off:off + 512],
                                        start=False, stop=True)
                                inst.then_inc(tv.h, 1)
                            bd.em("tensor", tqk)

                            sE = sv.n + 1
                            sv.n = sE
                            exp_t[kc] = sE

                            def sexp(e, kc=kc, tS=tS, pvf=pv_t.get(kc - 2, 0)):
                                e.wait_ge(tv.h, tS)
                                if pvf:
                                    e.wait_ge(tv.h, pvf)
                                e.activation(pt[:, kc % 2], ps_f[:, kc % 2, :],
                                             AF.Exp, bias=st["c_neg10"][:],
                                             scale=0.125).then_inc(sv.h, 1)
                            bd.em("scalar", sexp)

                            tP = tv.n + 1
                            tv.n = tP
                            pv_t[kc] = tP

                            def tpv(e, kc=kc, hh=hh, w=w, nkc=nkc, sE=sE,
                                    pvf=pvfree[0] if kc == 0 else 0):
                                e.wait_ge(sv.h, sE)
                                if kc == 0 and pvf:
                                    e.wait_ge(vv.h, pvf)
                                e.matmul(
                                    ps_f[0:65, 4, :],
                                    vtok[:, kc * 130 + hh * 65:
                                         kc * 130 + hh * 65 + 65],
                                    pt[:, kc % 2],
                                    start=(kc == 0),
                                    stop=(kc == nkc - 1)).then_inc(tv.h, 1)
                            bd.em("tensor", tpv)
                            last_pv = tP

                        vD = vv.n + 1
                        vv.n = vD

                        def vrec(e, last_pv=last_pv):
                            e.wait_ge(tv.h, last_pv)
                            e.reciprocal(recipb[64:65, :],
                                         ps_f[64:65, 4, :]).then_inc(vv.h, 1)
                        bd.em("vector", vrec)

                        tR = tv.n + 1
                        tv.n = tR

                        def trep(e, vD=vD):
                            e.wait_ge(vv.h, vD)
                            e.matmul(ps_f[0:64, 3, :],
                                     st["onesall"][64:65, 0:64],
                                     recipb[64:65, :], start=True,
                                     stop=True).then_inc(tv.h, 1)
                        bd.em("tensor", trep)

                        vN = vv.n + 2
                        vv.n = vN

                        def vnorm(e, w=w, tR=tR, vN=vN):
                            e.wait_ge(tv.h, tR)
                            e.tensor_copy(recipb[0:64, :],
                                          ps_f[0:64, 3, :]).then_inc(vv.h, 1)
                            e.wait_ge(vv.h, vN - 1)
                            e.tensor_tensor(
                                aoh[0:64, w * 512:(w + 1) * 512],
                                ps_f[0:64, 4, :], recipb[0:64, :],
                                ALU.mult).then_inc(vv.h, 1)
                        bd.em("vector", vnorm)
                        pvfree[0] = vN

                    tA = qg.tok()
                    vneed_h = vv.n

                    def gacc(e, head=head, tA=tA, vneed_h=vneed_h):
                        e.wait_ge(vv.h, vneed_h)
                        e.dma_start(
                            out=dm["ACC"].ap()[head // 2,
                                               (head % 2) * 64:
                                               (head % 2) * 64 + 64, :],
                            in_=aoh[0:64, :]).then_inc(tA[0].h, 16)
                    bd.em("gpsimd", gacc)
                    accfree[0] = tA
            bd.barrier()

            bd.phase_reset()
            acc_sb = bd.aa(f"{tag}_acc{b}", [128, ND, S], F32R)
            wo_sb = bd.aa(f"{tag}_wo{b}", [128, ND, D], F32R)
            wob_b = bd.aa(f"{tag}_wob{b}", [128, D], F32)
            xio = bd.aa(f"{tag}_xio{b}", [128, 2, D], F32)
            xoo = bd.aa(f"{tag}_xoo{b}", [128, 2, D], F32)
            tmpo = bd.aa(f"{tag}_tmpo{b}", [128, D], F32)

            tacc = qs.tok()
            two = qs.tok()
            twb = qs.tok()

            def dmoa(e, tacc=tacc, two=two, twb=twb):
                e.dma_start(out=acc_sb[:],
                            in_=dm["ACC"].ap().transpose([1, 0, 2])
                            ).then_inc(tacc[0].h, 16)
                e.dma_start(out=wo_sb[:],
                            in_=dm["wo"].ap().transpose([1, 0, 2])
                            ).then_inc(two[0].h, 16)
                e.dma_start(out=wob_b[:],
                            in_=bass.AP(dm["wob"], 0, [[0, 128], [1, D]])
                            ).then_inc(twb[0].h, 16)
            bd.em("sync", dmoa)

            ofree = [0, 0]
            oxfree = [0, 0]
            for tb in range(TPB):
                par = tb % 2
                txo = qs.tok()

                def dmxo(e, tb=tb, par=par, tb0=tb0, txo=txo,
                         xf=oxfree[par], src=src):
                    if xf:
                        e.wait_ge(vv.h, xf)
                    e.dma_start(out=xio[:, par], in_=src.ap()[tb0 + tb]
                                ).then_inc(txo[0].h, 16)
                bd.em("sync", dmxo)

                tO = tv.n
                tv.n = tO + 1

                def toproj2(e, tb=tb, par=par, tacc=tacc, two=two,
                            pf=ofree[par]):
                    W(e, tacc)
                    W(e, two)
                    if pf:
                        e.wait_ge(vv.h, pf)
                    inst = None
                    for fc in range(ND):
                        e.matmul(ps_f[:, par, :],
                                 acc_sb[:, fc, tb * 128:(tb + 1) * 128],
                                 wo_sb[:, fc, 0:512],
                                 start=(fc == 0), stop=(fc == ND - 1))
                        inst = e.matmul(ps_f[:, 2 + par, :],
                                        acc_sb[:, fc, tb * 128:(tb + 1) * 128],
                                        wo_sb[:, fc, 512:1024],
                                        start=(fc == 0), stop=(fc == ND - 1))
                    inst.then_inc(tv.h, 1)
                bd.em("tensor", toproj2)

                vO = vv.n + 4
                vv.n = vO

                def voproj(e, tb=tb, par=par, tO=tO, txo=txo, twb=twb,
                           gf=oxoutfree[par], vO=vO):
                    e.wait_ge(tv.h, tO + 1)
                    W(e, txo)
                    W(e, twb)
                    if gf:
                        W(e, gf)
                    for half in range(2):
                        sl = slice(half * 512, (half + 1) * 512)
                        bank = par if half == 0 else 2 + par
                        e.tensor_tensor(tmpo[:, sl], ps_f[:, bank, :],
                                        wob_b[:, sl],
                                        ALU.add).then_inc(vv.h, 1)
                        e.wait_ge(vv.h, vO - 3 + 2 * half)
                        e.tensor_tensor(xoo[:, par, sl], tmpo[:, sl],
                                        xio[:, par, sl],
                                        ALU.add).then_inc(vv.h, 1)
                bd.em("vector", voproj)
                ofree[par] = vO
                oxfree[par] = vO

                tgo = qg.tok()
                vneed_o = vv.n

                def gxo(e, tb=tb, par=par, tb0=tb0, vneed_o=vneed_o, tgo=tgo,
                        dst=dst):
                    e.wait_ge(vv.h, vneed_o)
                    e.dma_start(out=dst.ap()[tb0 + tb], in_=xoo[:, par]
                                ).then_inc(tgo[0].h, 16)
                bd.em("gpsimd", gxo)
                oxoutfree[par] = tgo
            bd.barrier()
            bd.phase_reset(keepz)

    prologue()
    if STAGE == 0:
        pass
    elif STAGE == 1:
        ffn("f1", dm["x"], dm["out"], 0)
    elif STAGE == 2:
        ffn("f1", dm["x"], dm["X"], 0)
        attn("a1", dm["X"], dm["out"])
    else:
        ffn("f1", dm["x"], dm["X"], 0)
        attn("a1", dm["X"], dm["X"])
        ffn("f2", dm["X"], dm["out"], 2)

    if DBG == 3:
        toks = [qg.tok() for _ in range(6)]

        def gdbg3(e, toks=tuple(toks)):
            dbg = dm["dbgout"].ap()
            e.dma_start(out=dbg[:, 0:8], in_=st["sc_bank"][:]
                        ).then_inc(toks[0][0].h, 16)
            e.dma_start(out=dbg[0:1, 8:16], in_=st["ccres"][:]
                        ).then_inc(toks[1][0].h, 16)
            e.dma_start(out=dbg[:, 16:48], in_=st["negm"][:]
                        ).then_inc(toks[2][0].h, 16)
            e.dma_start(out=dbg[:, 48:80], in_=st["rstd"][:]
                        ).then_inc(toks[3][0].h, 16)
            e.dma_start(out=dbg[:, 80:1104], in_=dm["G2"].ap()[0, :, 0:1024]
                        ).then_inc(toks[4][0].h, 16)
            e.dma_start(out=dbg[:, 1104:1616].bitcast(BF16),
                        in_=dm["WGT"].ap()[0, :, :]
                        ).then_inc(toks[5][0].h, 16)
            for tk in toks:
                W(e, tk)
        bd.em("gpsimd", gdbg3)

    gsnap = tuple(qg.snapshot())
    ssnap = tuple(qs.snapshot())

    def tail_g(e, snap=gsnap):
        for s, v in snap:
            e.wait_ge(s.h, v)
    bd.em("gpsimd", tail_g)

    def tail_s(e, snap=ssnap):
        for s, v in snap:
            e.wait_ge(s.h, v)
    bd.em("sync", tail_s)

    with nc.Block() as block:
        @block.sync
        def _(e):
            for f in bd.code["sync"]:
                f(e)

        @block.vector
        def _(e):
            for f in bd.code["vector"]:
                f(e)

        @block.scalar
        def _(e):
            for f in bd.code["scalar"]:
                f(e)

        @block.tensor
        def _(e):
            for f in bd.code["tensor"]:
                f(e)

        @block.gpsimd
        def _(e):
            for f in bd.code["gpsimd"]:
                f(e)

    es.close()
    return nc


def _host_consts():
    inv = 1.0 / (10000.0 ** (np.arange(0, HD, 2)[: HD // 2].astype(np.float64)
                             / HD))
    ang = np.outer(np.arange(S, dtype=np.float64), inv)
    cosv = np.cos(ang).astype(np.float32)
    sinv = np.sin(ang).astype(np.float32)
    t1 = np.zeros((128, S), np.float32)
    t2 = np.zeros((128, S), np.float32)
    for blk in range(4):
        rows = slice(blk * 32, blk * 32 + 32)
        t1[rows] = cosv.T
        t2[rows] = sinv.T if blk % 2 == 1 else -sinv.T
    idbf = np.eye(128, dtype=ml_dtypes.bfloat16)
    idr = np.eye(128, dtype=np.float32)
    cc = np.arange(1024)[None, :]
    kk = np.arange(128)[:, None]
    tri = np.where(cc < kk + 384, np.float32(NEGBIG),
                   np.float32(0.0)).astype(np.float32)
    return t1, t2, idbf, idr, tri


def _perm_cols():
    perm = []
    for h in range(H):
        perm += [h * HD + i for i in range(0, HD, 2)]
        perm += [h * HD + i for i in range(1, HD, 2)]
    return np.array(perm)


def kernel(L0_attn_x, L0_attn_wq, L0_attn_wk, L0_attn_wv, L0_attn_wo_w,
           L0_attn_wo_b, L0_ffn_clip_x, L0_ffn_clip_down, L0_ffn_w_gate,
           L0_ffn_w_up, L0_ffn_w_down):
    x = np.ascontiguousarray(np.asarray(L0_attn_x, dtype=np.float32))
    wq = np.asarray(L0_attn_wq, np.float32)
    wk = np.asarray(L0_attn_wk, np.float32)
    wv = np.asarray(L0_attn_wv, np.float32)
    wo = np.asarray(L0_attn_wo_w, np.float32)
    wob = np.asarray(L0_attn_wo_b, np.float32).reshape(1, D)
    wg = np.ascontiguousarray(np.asarray(L0_ffn_w_gate, np.float32))
    wu = np.ascontiguousarray(np.asarray(L0_ffn_w_up, np.float32))
    wd = np.ascontiguousarray(np.asarray(L0_ffn_w_down, np.float32))
    clip = np.array([[float(np.asarray(L0_ffn_clip_x).reshape(-1)[0]),
                      float(np.asarray(L0_ffn_clip_down).reshape(-1)[0])]],
                    np.float32)

    perm = _perm_cols()
    wq_p, wk_p = wq[:, perm], wk[:, perm]

    def pair_tile(w):
        return np.ascontiguousarray(
            w.reshape(ND, 128, NPAIR, 128).transpose(2, 0, 1, 3))

    wq_t, wk_t, wv_t = pair_tile(wq_p), pair_tile(wk_p), pair_tile(wv)
    wo_t = np.ascontiguousarray(wo.reshape(ND, 128, D))
    t1, t2, idbf, idr, tri = _host_consts()

    nc = build_program()
    common = dict(wg=wg, wu=wu, wd=wd, clip=clip, idbf=idbf)
    if STAGE >= 2:
        common.update(wq=wq_t, wk=wk_t, wv=wv_t, wo=wo_t, wob=wob, t1=t1,
                      t2=t2, idr=idr, trimask=tri)
    in_maps = []
    for c in range(NCORE):
        xc = x[c * BC:(c + 1) * BC].reshape(NT, 128, D)
        in_maps.append(dict(common, x=np.ascontiguousarray(xc)))
    res = run_bass_kernel_spmd(nc, in_maps, core_ids=list(range(NCORE)))
    global LAST_RESULTS
    LAST_RESULTS = res
    outs = [r["out"].reshape(BC, S, D) for r in res.results]
    return np.concatenate(outs, axis=0)
